# version 08
# baseline (speedup 1.0000x reference)
import numpy as np
from contextlib import ExitStack

import concourse.bass as bass
import concourse.mybir as mybir
from concourse.bass_utils import run_bass_kernel_spmd

F32 = mybir.dt.float32
BF16 = mybir.dt.bfloat16
AF = mybir.ActivationFunctionType
ALU = mybir.AluOpType
AX = mybir.AxisListType

D = 2048
KC = D // 128
HEADS = 16
HD = 128
NE = 8
H = 1408
HC = H // 128
HS = 2816
RSCALE = float(1.0 / np.sqrt(HD))
SRMS2 = 1.0 / HD
LN_EPS = 1e-5

ENGINES = ("tensor", "vector", "scalar", "gpsimd", "sync")


class KB:

    ENGINE_SEMS = ("pe", "v", "a")

    def __init__(self, nc, sems):
        self.nc = nc
        self.sems = sems
        self.cnt = {k: 0 for k in sems}
        self.ops = {e: [] for e in ENGINES}
        self.waited = {e: {} for e in ENGINES}
        self.max_waited = {}

    def op(self, eng, fn, deps=(), inc=None, step=1):
        lst = self.ops[eng]
        w = self.waited[eng]
        for d in deps:
            if d is None:
                continue
            s, v = d
            assert v <= self.cnt[s], (s, v, self.cnt[s])
            if w.get(s, 0) < v:
                sem = self.sems[s]
                lst.append(lambda E, sem=sem, v=v: E.wait_ge(sem, v))
                w[s] = v
                if v > self.max_waited.get(s, 0):
                    self.max_waited[s] = v
        if inc is None:
            lst.append(fn)
            return None
        self.cnt[inc] += step
        tok = (inc, self.cnt[inc])
        sem = self.sems[inc]
        lst.append(lambda E, fn=fn, sem=sem, step=step: fn(E).then_inc(sem, step))
        return tok

    def dma(self, eng, out, in_, deps=(), sem=None):
        return self.op(eng, lambda E: E.dma_start(out=out, in_=in_), deps,
                       inc=sem, step=16)

    def tok(self, s):
        return (s, self.cnt[s])

    def flush(self, skip=()):
        for s in self.cnt:
            if s not in self.ENGINE_SEMS and s not in skip and \
                    self.cnt[s] > self.max_waited.get(s, 0):
                self.op("sync", lambda E: E.nop(), deps=[(s, self.cnt[s])])
        ops = self.ops
        with self.nc.Block() as blk:
            for e in ENGINES:
                if ops[e]:
                    def runner(E, seq=tuple(ops[e])):
                        for f in seq:
                            f(E)
                    getattr(blk, e)(runner)
        self.ops = {e: [] for e in ENGINES}
        self.waited = {e: dict(self.cnt) for e in ENGINES}
        self.max_waited = dict(self.cnt)


class PPool:

    def __init__(self, tensors):
        self.t = tensors
        self.cur = [0] * len(tensors)
        self.ctok = [[None] * n for (_, n) in tensors]
        self.rr = 0

    def claim(self, n):
        for _ in range(len(self.t)):
            ti = self.rr % len(self.t)
            self.rr += 1
            handle, nslots = self.t[ti]
            if nslots < n:
                continue
            if self.cur[ti] + n > nslots:
                self.cur[ti] = 0
            off = self.cur[ti]
            self.cur[ti] += n
            deps = [self.ctok[ti][off + j] for j in range(n)]
            return (ti, off, n, handle, deps)
        raise RuntimeError("no psum tensor big enough")

    def ap(self, grp, c0=0, c1=None):
        ti, off, n, handle, _ = grp
        base = off * 512
        if c1 is None:
            c1 = n * 512
        return handle[:, base + c0: base + c1]

    def release(self, grp, tok):
        ti, off, n, _, _ = grp
        for j in range(n):
            self.ctok[ti][off + j] = tok


STAT_NAMES = ("s1n", "s2", "mean_n", "msq", "ep", "lnv", "rstd", "nmr")


def _ln_xn(kb, xin_3d, xout_3d, nsubs, sq_scr, stats, x_tok, prev, war_out, cst):
    t_red = kb.op("vector", lambda E: E.reduce_sum(
        stats["s1n"], xin_3d, axis=AX.X, negate=True),
        deps=[x_tok, prev], inc="v")
    sq_tok = None
    for s in range(nsubs):
        sq_tok = kb.op("scalar", lambda E, s=s: E.activation(
            sq_scr[:, :], xin_3d[:, s, :], AF.Square,
            accum_out=stats["s2"][:, s: s + 1]),
            deps=[x_tok, prev] if s == 0 else [], inc="a")
    t1 = kb.op("vector", lambda E: E.tensor_scalar(
        stats["mean_n"], stats["s1n"], 1.0 / D, None, ALU.mult),
        deps=[t_red], inc="v")
    t2 = kb.op("vector", lambda E: E.tensor_tensor(
        stats["msq"], stats["mean_n"], stats["mean_n"], ALU.mult),
        deps=[t1], inc="v")
    t3 = kb.op("vector", lambda E: E.tensor_scalar(
        stats["ep"], stats["s2"], 1.0 / D, None, ALU.mult),
        deps=[sq_tok], inc="v")
    t4 = kb.op("vector", lambda E: E.tensor_tensor(
        stats["ep"], stats["ep"], stats["msq"], ALU.subtract),
        deps=[t2, t3], inc="v")
    a1 = kb.op("scalar", lambda E: E.activation(
        stats["lnv"], stats["ep"], AF.Ln, bias=cst["eps"]), deps=[t4], inc="a")
    a2 = kb.op("scalar", lambda E: E.activation(
        stats["rstd"], stats["lnv"], AF.Exp, scale=cst["mhalf"]), deps=[a1], inc="a")
    t5 = kb.op("vector", lambda E: E.tensor_tensor(
        stats["nmr"], stats["mean_n"], stats["rstd"], ALU.mult),
        deps=[a2], inc="v")
    xn_tok = None
    for s in range(nsubs):
        xn_tok = kb.op("vector", lambda E, s=s: E.tensor_scalar(
            xout_3d[:, s, :], xin_3d[:, s, :],
            stats["rstd"][:, s: s + 1], stats["nmr"][:, s: s + 1],
            ALU.mult, ALU.add),
            deps=([t5, sq_tok] + list(war_out)) if s == 0 else [], inc="v")
    return xn_tok


def build_program(T, groups):
    nc = bass.Bass("TRN2", target_bir_lowering=False)
    NSUB = T // 128
    N1 = max(1, T // 512)
    P3T = 1024 if T % 1024 == 0 else 512
    N3 = T // P3T
    S3 = P3T // 128
    NT3 = P3T // 512

    x_d = nc.dram_tensor("x", [T, D], F32, kind="ExternalInput")
    wqT_d = nc.dram_tensor("wqT", [D, D], F32, kind="ExternalInput")
    wkT_d = nc.dram_tensor("wkT", [D, D], F32, kind="ExternalInput")
    wvT_d = nc.dram_tensor("wvT", [D, D], F32, kind="ExternalInput")
    wuT_d = nc.dram_tensor("wuT", [D, D], F32, kind="ExternalInput")
    woT_d = nc.dram_tensor("woT", [D, D], F32, kind="ExternalInput")
    gwT_d = nc.dram_tensor("gwT", [D, 16], F32, kind="ExternalInput")
    wge_d = nc.dram_tensor("wge", [NE, D, H], F32, kind="ExternalInput")
    wue_d = nc.dram_tensor("wue", [NE, D, H], F32, kind="ExternalInput")
    wde_d = nc.dram_tensor("wde", [NE, H, D], F32, kind="ExternalInput")
    shwg_d = nc.dram_tensor("shwg", [D, HS], F32, kind="ExternalInput")
    shwu_d = nc.dram_tensor("shwu", [D, HS], F32, kind="ExternalInput")
    shwd_d = nc.dram_tensor("shwd", [HS, D], F32, kind="ExternalInput")
    id_d = nc.dram_tensor("identw", [128, 128], F32, kind="ExternalInput")
    out_d = nc.dram_tensor("out", [T, D], F32, kind="ExternalOutput")
    kvp_d = nc.dram_tensor("kv_part", [128, D], F32, kind="Internal")
    kvf_d = nc.dram_tensor("kv_full", [128, D], F32, kind="Internal")
    xn_d = nc.dram_tensor("xn_sav", [N1, KC, 128, 512], BF16, kind="Internal")

    xr = x_d.ap().rearrange("(a p) d -> p a d", p=128)
    outr = out_d.ap().rearrange("(a p) d -> p a d", p=128)

    with ExitStack() as top:
        ec = top.enter_context
        sems = {}
        for s in ("pe", "v", "a", "scc", "swc",
                  "sx0", "sx1", "sw0", "sw1", "sw2", "sw3",
                  "so0", "so1", "sacc", "skv", "sxn", "sxl"):
            sems[s] = ec(nc.semaphore(s))
        kb = KB(nc, sems)

        ident = ec(nc.sbuf_tensor("ident", [128, 128], F32))
        c_all = ec(nc.sbuf_tensor("c_all", [128, NSUB, NE], F32))
        sg_all = ec(nc.sbuf_tensor("sg_all", [128, NSUB], F32))
        gw_sb = ec(nc.sbuf_tensor("gw_sb", [128, KC, 16], BF16))
        cst_t = ec(nc.sbuf_tensor("cst", [128, 8], F32))
        cst = {n: cst_t[:, i:i + 1] for i, n in enumerate(
            ("eps", "tiny", "mhalf", "srms2", "mone"))}

        with ExitStack() as p1:
            e1 = p1.enter_context
            xbuf = e1(nc.sbuf_tensor("p1_x", [128, 2, 4, D], F32))
            sq_scr = e1(nc.sbuf_tensor("p1_sq", [128, D], BF16))
            st_t = e1(nc.sbuf_tensor("p1_st", [128, 8, 4], F32))
            stt = {n: st_t[:, i, :] for i, n in enumerate(STAT_NAMES)}
            xn_sb = e1(nc.sbuf_tensor("p1_xn", [128, KC, 512], BF16))
            k_sb = e1(nc.sbuf_tensor("p1_k", [128, 4, D], BF16))
            v_sb = e1(nc.sbuf_tensor("p1_v", [128, 4, D], BF16))
            wbuf = e1(nc.sbuf_tensor("p1_w", [128, 2, 2, KC, 512], BF16))
            kvs = e1(nc.sbuf_tensor("p1_kvs", [128, D], F32))
            ppa = e1(nc.psum_tensor("p1_ppa", [128, 2048], F32))
            ppb = e1(nc.psum_tensor("p1_ppb", [128, 2048], F32))
            pool = PPool([(ppa, 4), (ppb, 4)])

            kv_zero = kb.op("vector", lambda E: E.memset(kvs[:, :], 0.0),
                            inc="v")
            cst_tok = None
            for ci, val in enumerate((LN_EPS, 1e-30, -0.5, SRMS2, -1.0)):
                cst_tok = kb.op("vector", lambda E, ci=ci, val=val: E.memset(
                    cst_t[:, ci:ci + 1], val), inc="v")
            kb.dma("gpsimd", ident[:, :], id_d[:, :], sem="swc")
            gwr = gwT_d.ap().rearrange("(k p) e -> p k e", p=128)
            gw_tok = kb.dma("gpsimd", gw_sb[:, :, :], gwr, sem="swc")
            ident_tok = gw_tok

            wkr = wkT_d.ap().rearrange("(k p) f -> p k f", p=128)
            wvr = wvT_d.ap().rearrange("(k p) f -> p k f", p=128)

            NFG = 4
            nload = N1 * NFG
            wl_tok = [None] * nload
            wl_war = [None] * nload

            def load_w1(j):
                fg = j % NFG
                par = j % 2
                dep = [wl_war[j - 2]] if j >= 2 else []
                kb.dma("gpsimd", wbuf[:, par, 0, :, :],
                       wkr[:, :, fg * 512:(fg + 1) * 512], deps=dep,
                       sem="sw%d" % par)
                wl_tok[j] = kb.dma("gpsimd", wbuf[:, par, 1, :, :],
                                   wvr[:, :, fg * 512:(fg + 1) * 512],
                                   sem="sw%d" % par)

            x_tok = [None] * N1
            x_free = [None] * N1

            def load_x1(i):
                dep = [x_free[i - 2]] if i >= 2 else []
                x_tok[i] = kb.dma("sync", xbuf[:, i % 2, :, :],
                                  xr[:, 4 * i:4 * i + 4, :], deps=dep,
                                  sem="sx%d" % (i % 2))

            load_x1(0)
            if N1 > 1:
                load_x1(1)
            load_w1(0)
            if nload > 1:
                load_w1(1)

            ln_tok = [None] * (N1 + 1)
            ln_tok[0] = _ln_xn(kb, xbuf[:, 0, :, :], xbuf[:, 0, :, :], 4,
                               sq_scr, stt, x_tok[0], cst_tok, [], cst)

            xn_store = None
            prev_xn_reader = None
            prev_kv_done = None
            kv_acc = [kv_zero] * 4
            for i in range(N1):
                par = i % 2
                xnf_tok = None
                for kc in range(KC):
                    g = pool.claim(1)
                    tr = None
                    for s in range(4):
                        dep = ([ln_tok[i], ident_tok] + g[4]) if s == 0 else []
                        tr = kb.op("tensor", lambda E, kc=kc, s=s, g=g, par=par: E.transpose(
                            pool.ap(g, s * 128, (s + 1) * 128),
                            xbuf[:, par, s, kc * 128:(kc + 1) * 128],
                            ident[:, :]), deps=dep,
                            inc="pe" if s == 3 else None)
                    xnf_tok = kb.op("scalar", lambda E, kc=kc, g=g: E.activation(
                        xn_sb[:, kc, :], pool.ap(g), AF.Copy),
                        deps=[tr, prev_xn_reader, xn_store], inc="a")
                    pool.release(g, xnf_tok)
                x_free[i] = kb.tok("pe")
                if i + 2 < N1:
                    load_x1(i + 2)
                xn_store = kb.dma(
                    "sync", xn_d.ap()[i].rearrange("k p t -> p k t"),
                    xn_sb[:, :, :], deps=[xnf_tok], sem="sxn")

                if i + 1 < N1:
                    ln_tok[i + 1] = _ln_xn(
                        kb, xbuf[:, (i + 1) % 2, :, :],
                        xbuf[:, (i + 1) % 2, :, :], 4,
                        sq_scr, stt, x_tok[i + 1], ln_tok[i], [], cst)

                evac_tok = None
                for fg in range(NFG):
                    j = i * NFG + fg
                    for m in range(2):
                        for s in range(4):
                            g = pool.claim(1)
                            mm = None
                            for kc in range(KC):
                                dep = ([wl_tok[j], xnf_tok] + g[4]) if kc == 0 else []
                                mm = kb.op("tensor", lambda E, kc=kc, s=s, m=m, g=g, par2=j % 2: E.matmul(
                                    pool.ap(g),
                                    xn_sb[:, kc, s * 128:(s + 1) * 128],
                                    wbuf[:, par2, m, kc, :],
                                    start=(kc == 0), stop=(kc == KC - 1)),
                                    deps=dep, inc="pe" if kc == KC - 1 else None)
                            if m == 0:
                                evac_tok = kb.op("vector", lambda E, s=s, fg=fg, g=g: E.tensor_scalar(
                                    k_sb[:, s, fg * 512:(fg + 1) * 512],
                                    pool.ap(g), 0.0, RSCALE, ALU.max, ALU.mult),
                                    deps=[mm, prev_kv_done], inc="v")
                            else:
                                evac_tok = kb.op("vector", lambda E, s=s, fg=fg, g=g: E.tensor_copy(
                                    v_sb[:, s, fg * 512:(fg + 1) * 512],
                                    pool.ap(g)),
                                    deps=[mm, prev_kv_done], inc="v")
                            pool.release(g, evac_tok)
                    wl_war[j] = kb.tok("pe")
                    if j + 2 < nload:
                        load_w1(j + 2)
                prev_xn_reader = kb.tok("pe")

                for hg in range(4):
                    g = pool.claim(1)
                    mm = None
                    for hq in range(4):
                        h = hg * 4 + hq
                        for s in range(4):
                            dep = ([evac_tok] + g[4]) if (hq == 0 and s == 0) else []
                            mm = kb.op("tensor", lambda E, s=s, h=h, hq=hq, g=g: E.matmul(
                                pool.ap(g, hq * 128, (hq + 1) * 128),
                                k_sb[:, s, h * 128:(h + 1) * 128],
                                v_sb[:, s, h * 128:(h + 1) * 128],
                                start=(s == 0), stop=(s == 3)),
                                deps=dep,
                                inc="pe" if (hq == 3 and s == 3) else None)
                    kv_acc[hg] = kb.op("vector", lambda E, hg=hg, g=g: E.tensor_tensor(
                        kvs[:, hg * 512:(hg + 1) * 512],
                        kvs[:, hg * 512:(hg + 1) * 512],
                        pool.ap(g), ALU.add),
                        deps=[mm, kv_acc[hg]], inc="v")
                    pool.release(g, kv_acc[hg])
                prev_kv_done = kb.tok("pe")

            t2 = kb.dma("scalar", kvp_d[:, :], kvs[:, :],
                        deps=[kv_acc[0], kv_acc[1], kv_acc[2], kv_acc[3]],
                        sem="skv")
            t3 = kb.op("gpsimd", lambda E: E.collective_compute(
                "AllReduce", ALU.add, replica_groups=groups,
                ins=[kvp_d.ap().opt()], outs=[kvf_d.ap().opt()]),
                deps=[t2], inc="scc")
            kb.flush(skip=("scc",))

        with ExitStack() as p2:
            e2 = p2.enter_context
            xbuf = e2(nc.sbuf_tensor("p2_x", [128, 2, 4, D], F32))
            o_sb = e2(nc.sbuf_tensor("p2_osb", [128, 4, D], F32))
            kv_bf = e2(nc.sbuf_tensor("p2_kvb", [128, D], BF16))
            xn_sb = e2(nc.sbuf_tensor("p2_xn", [128, KC, 512], BF16))
            u_sb = e2(nc.sbuf_tensor("p2_u", [128, 4, D], BF16))
            qof = e2(nc.sbuf_tensor("p2_qof", [128, 16, 512], BF16))
            wsl = e2(nc.sbuf_tensor("p2_w", [128, 3, KC, 512], BF16))
            log_sb = e2(nc.sbuf_tensor("p2_log", [128, 4, 16], F32))
            gsc = e2(nc.sbuf_tensor("p2_gsc", [128, 4, 32], F32))
            srms_scr = e2(nc.sbuf_tensor("p2_ss", [128, D], BF16))
            ssq = e2(nc.sbuf_tensor("p2_ssq", [128, 16], F32))
            rn = e2(nc.sbuf_tensor("p2_rn", [128, 16], F32))
            ppa = e2(nc.psum_tensor("p2_ppa", [128, 2048], F32))
            ppb = e2(nc.psum_tensor("p2_ppb", [128, 2048], F32))
            pool = PPool([(ppa, 4), (ppb, 4)])

            wqr = wqT_d.ap().rearrange("(k p) f -> p k f", p=128)
            wur = wuT_d.ap().rearrange("(k p) f -> p k f", p=128)
            wor = woT_d.ap().rearrange("(k p) f -> p k f", p=128)

            NSL = 12
            nload = N1 * NSL
            wl_tok = [None] * nload
            wl_war = [None] * nload

            def load_w2(j):
                sl = j % NSL
                par = j % 3
                src = (wqr, wur, wor)[sl // 4]
                fg = sl % 4
                dep = [wl_war[j - 3]] if j >= 3 else []
                wl_tok[j] = kb.dma("gpsimd", wsl[:, par, :, :],
                                   src[:, :, fg * 512:(fg + 1) * 512],
                                   deps=dep, sem="sw%d" % par)

            x_tok = [None] * N1
            x_free = [None] * N1

            def load_x2(i):
                dep = list(x_free[i - 2]) if i >= 2 else []
                x_tok[i] = kb.dma("sync", xbuf[:, i % 2, :, :],
                                  xr[:, 4 * i:4 * i + 4, :], deps=dep,
                                  sem="sx%d" % (i % 2))

            kv_rdy = None

            load_x2(0)
            if N1 > 1:
                load_x2(1)
            for j in range(3):
                load_w2(j)

            xn_ld = [None] * N1

            def load_xn2(i):
                dep = [prev_xn_reader]
                if i == 0:
                    dep.append(kb.tok("sxn"))
                xn_ld[i] = kb.dma("sync", xn_sb[:, :, :],
                                  xn_d.ap()[i].rearrange("k p t -> p k t"),
                                  deps=dep, sem="sxl")

            prev_xn_reader = None
            prev_gate = None
            prev_u_reader = None
            prev_qof_o_reader = None
            prev_obf_reader = None
            last_store = [None, None]
            load_xn2(0)
            for i in range(N1):
                par = i % 2
                xnf_tok = xn_ld[i]

                lg_evac = None
                for s in range(4):
                    g = pool.claim(1)
                    mm = None
                    for kc in range(KC):
                        dep = ([xnf_tok, gw_tok] + g[4]) if kc == 0 else []
                        mm = kb.op("tensor", lambda E, kc=kc, s=s, g=g: E.matmul(
                            pool.ap(g, 0, 16),
                            xn_sb[:, kc, s * 128:(s + 1) * 128],
                            gw_sb[:, kc, :],
                            start=(kc == 0), stop=(kc == KC - 1)),
                            deps=dep, inc="pe" if kc == KC - 1 else None)
                    lg_evac = kb.op("vector", lambda E, s=s, g=g: E.tensor_copy(
                        log_sb[:, s, :], pool.ap(g, 0, 16)),
                        deps=[mm, prev_gate], inc="v")
                    pool.release(g, lg_evac)
                e_v = gsc[:, :, 0:8]
                im_v = gsc[:, :, 8:16]
                e2_v = gsc[:, :, 16:24]
                m1_v = gsc[:, :, 24:25].squeeze(2)
                m2_v = gsc[:, :, 25:26].squeeze(2)
                den_v = gsc[:, :, 26:27].squeeze(2)
                ez_v = gsc[:, :, 27:28].squeeze(2)
                g1 = kb.op("scalar", lambda E: E.activation(
                    e_v, log_sb[:, :, 0:8], AF.Exp), deps=[lg_evac], inc="a")
                g2 = kb.op("vector", lambda E: E.reduce_max(
                    m1_v, e_v, axis=AX.X), deps=[g1], inc="v")
                g3 = kb.op("vector", lambda E: E.tensor_tensor(
                    im_v, e_v, m1_v.unsqueeze(2).broadcast_to([128, 4, 8]),
                    ALU.is_ge), deps=[g2], inc="v")
                g4 = kb.op("vector", lambda E: E.tensor_scalar(
                    im_v, im_v, -1.0, 1.0, ALU.mult, ALU.add), deps=[g3], inc="v")
                g5 = kb.op("vector", lambda E: E.tensor_tensor(
                    e2_v, e_v, im_v, ALU.mult), deps=[g4], inc="v")
                g6 = kb.op("vector", lambda E: E.reduce_max(
                    m2_v, e2_v, axis=AX.X), deps=[g5], inc="v")
                g7 = kb.op("vector", lambda E: E.tensor_tensor(
                    im_v, e_v, m2_v.unsqueeze(2).broadcast_to([128, 4, 8]),
                    ALU.is_ge), deps=[g6], inc="v")
                g8 = kb.op("vector", lambda E: E.tensor_tensor(
                    den_v, m1_v, m2_v, ALU.add), deps=[g6], inc="v")
                g9 = kb.op("vector", lambda E: E.reciprocal(
                    den_v, den_v), deps=[g8], inc="v")
                g10 = kb.op("vector", lambda E: E.tensor_tensor(
                    e2_v, e_v, im_v, ALU.mult), deps=[g7], inc="v")
                g11 = kb.op("vector", lambda E, i=i: E.tensor_tensor(
                    c_all[:, 4 * i:4 * i + 4, :], e2_v,
                    den_v.unsqueeze(2).broadcast_to([128, 4, 8]), ALU.mult),
                    deps=[g9, g10], inc="v")
                sg1 = kb.op("scalar", lambda E: E.activation(
                    ez_v.unsqueeze(2), log_sb[:, :, 8:9], AF.Exp, scale=cst["mone"]),
                    deps=[lg_evac], inc="a")
                sg2 = kb.op("vector", lambda E: E.tensor_scalar(
                    ez_v, ez_v, 1.0, None, ALU.add), deps=[sg1], inc="v")
                sg3 = kb.op("vector", lambda E, i=i: E.reciprocal(
                    sg_all[:, 4 * i:4 * i + 4], ez_v), deps=[sg2], inc="v")
                prev_gate = sg3

                q_evac = None
                for hg in range(4):
                    j = i * NSL + hg
                    g = pool.claim(4)
                    mm = None
                    for hq in range(4):
                        for kc in range(KC):
                            dep = ([wl_tok[j], xnf_tok, prev_qof_o_reader] + g[4]) \
                                if (hq == 0 and kc == 0) else []
                            mm = kb.op("tensor", lambda E, kc=kc, hq=hq, g=g, par2=j % 3: E.matmul(
                                pool.ap(g, hq * 512, (hq + 1) * 512),
                                wsl[:, par2, kc, hq * 128:(hq + 1) * 128],
                                xn_sb[:, kc, :],
                                start=(kc == 0), stop=(kc == KC - 1)),
                                deps=dep,
                                inc="pe" if (hq == 3 and kc == KC - 1) else None)
                    q_evac = kb.op("vector", lambda E, hg=hg, g=g: E.tensor_scalar(
                        qof[:, 4 * hg:4 * hg + 4, :].rearrange("p a b -> p (a b)"),
                        pool.ap(g), 0.0, RSCALE, ALU.max, ALU.mult),
                        deps=[mm], inc="v")
                    pool.release(g, q_evac)
                    wl_war[j] = kb.tok("pe")
                    if j + 3 < nload:
                        load_w2(j + 3)

                u_tok = None
                for fg in range(4):
                    j = i * NSL + 4 + fg
                    for s in range(4):
                        g = pool.claim(1)
                        mm = None
                        for kc in range(KC):
                            dep = ([wl_tok[j], xnf_tok, prev_u_reader] + g[4]) \
                                if kc == 0 else []
                            mm = kb.op("tensor", lambda E, kc=kc, s=s, g=g, par2=j % 3: E.matmul(
                                pool.ap(g),
                                xn_sb[:, kc, s * 128:(s + 1) * 128],
                                wsl[:, par2, kc, :],
                                start=(kc == 0), stop=(kc == KC - 1)),
                                deps=dep, inc="pe" if kc == KC - 1 else None)
                        u_tok = kb.op("scalar", lambda E, s=s, fg=fg, g=g: E.activation(
                            u_sb[:, s, fg * 512:(fg + 1) * 512], pool.ap(g),
                            AF.Silu), deps=[mm], inc="a")
                        pool.release(g, u_tok)
                    wl_war[j] = kb.tok("pe")
                    if j + 3 < nload:
                        load_w2(j + 3)
                prev_xn_reader = kb.tok("pe")
                if i + 1 < N1:
                    load_xn2(i + 1)
                if i == 0:
                    kv_rdy = kb.dma("gpsimd", kv_bf[:, :], kvf_d[:, :],
                                    deps=[t3], sem="skv")

                op_tok = [None] * 4
                prev_t1 = None
                for s in range(4):
                    g = pool.claim(4)
                    mm = None
                    for h in range(HEADS):
                        dep = ([q_evac, kv_rdy] + g[4]) if h == 0 else []
                        mm = kb.op("tensor", lambda E, s=s, h=h, g=g: E.matmul(
                            pool.ap(g, h * 128, (h + 1) * 128),
                            qof[:, h, s * 128:(s + 1) * 128],
                            kv_bf[:, h * 128:(h + 1) * 128],
                            start=True, stop=True),
                            deps=dep, inc="pe" if h == HEADS - 1 else None)
                    o_cp = kb.op("vector", lambda E, s=s, g=g: E.tensor_copy(
                        o_sb[:, s, :], pool.ap(g)),
                        deps=[mm, prev_obf_reader, prev_u_reader], inc="v")
                    pool.release(g, o_cp)
                    o_sq = kb.op("scalar", lambda E, s=s: E.activation(
                        srms_scr[:, :], o_sb[:, s, :], AF.Square),
                        deps=[o_cp, prev_t1], inc="a")
                    o_red = kb.op("vector", lambda E: E.reduce_sum(
                        ssq[:, :],
                        srms_scr[:, :].rearrange("p (h e) -> p h e", e=128),
                        axis=AX.X), deps=[o_sq], inc="v")
                    o_ln = kb.op("scalar", lambda E: E.activation(
                        rn[:, :], ssq[:, :], AF.Ln, bias=cst["tiny"], scale=cst["srms2"]),
                        deps=[o_red], inc="a")
                    o_ex = kb.op("scalar", lambda E: E.activation(
                        rn[:, :], rn[:, :], AF.Exp, scale=cst["mhalf"]),
                        deps=[o_ln], inc="a")
                    o_mn = kb.op("vector", lambda E: E.tensor_scalar(
                        rn[:, :], rn[:, :], 1e12, None, ALU.min),
                        deps=[o_ex], inc="v")
                    t1_tok = kb.op("vector", lambda E, s=s: E.tensor_tensor(
                        o_sb[:, s, :].rearrange("p (h e) -> p h e", e=128),
                        o_sb[:, s, :].rearrange("p (h e) -> p h e", e=128),
                        rn[:, :].unsqueeze(2).broadcast_to([128, 16, 128]),
                        ALU.mult), deps=[o_mn], inc="v")
                    prev_t1 = t1_tok
                    op_tok[s] = kb.op("vector", lambda E, s=s: E.tensor_tensor(
                        o_sb[:, s, :], o_sb[:, s, :], u_sb[:, s, :], ALU.mult),
                        deps=[t1_tok, u_tok if s == 0 else None], inc="v")
                prev_u_reader = op_tok[3]

                of_a = [None] * 4
                of_v = [None] * 4
                for s in range(4):
                    for jj in range(4):
                        g = pool.claim(1)
                        tr = None
                        for q4 in range(4):
                            kc = jj * 4 + q4
                            dep = ([op_tok[s]] + g[4]) if q4 == 0 else []
                            tr = kb.op("tensor", lambda E, kc=kc, s=s, q4=q4, g=g: E.transpose(
                                pool.ap(g, q4 * 128, (q4 + 1) * 128),
                                o_sb[:, s, kc * 128:(kc + 1) * 128],
                                ident[:, :]), deps=dep,
                                inc="pe" if q4 == 3 else None)
                        if jj % 2 == 0:
                            of_a[s] = kb.op("scalar", lambda E, s=s, jj=jj, g=g: E.activation(
                                qof[:, 4 * jj:4 * jj + 4, s * 128:(s + 1) * 128],
                                pool.ap(g).rearrange("p (a b) -> p a b", b=128),
                                AF.Copy), deps=[tr], inc="a")
                            pool.release(g, of_a[s])
                        else:
                            of_v[s] = kb.op("vector", lambda E, s=s, jj=jj, g=g: E.tensor_copy(
                                qof[:, 4 * jj:4 * jj + 4, s * 128:(s + 1) * 128],
                                pool.ap(g).rearrange("p (a b) -> p a b", b=128)),
                                deps=[tr], inc="v")
                            pool.release(g, of_v[s])
                prev_obf_reader = kb.tok("pe")

                ev = [[None] * 4 for _ in range(4)]
                for fh in range(4):
                    j = i * NSL + 8 + fh
                    for s in range(4):
                        g = pool.claim(1)
                        mm = None
                        for kc in range(KC):
                            dep = ([wl_tok[j], of_a[s], of_v[s]] + g[4]) \
                                if kc == 0 else []
                            mm = kb.op("tensor", lambda E, kc=kc, s=s, g=g, par2=j % 3: E.matmul(
                                pool.ap(g),
                                qof[:, kc, s * 128:(s + 1) * 128],
                                wsl[:, par2, kc, :],
                                start=(kc == 0), stop=(kc == KC - 1)),
                                deps=dep, inc="pe" if kc == KC - 1 else None)
                        ev[s][fh] = kb.op("vector", lambda E, s=s, fh=fh, g=g, par=par: E.tensor_tensor(
                            xbuf[:, par, s, fh * 512:(fh + 1) * 512],
                            pool.ap(g),
                            xbuf[:, par, s, fh * 512:(fh + 1) * 512],
                            ALU.add),
                            deps=[mm, x_tok[i]], inc="v")
                        pool.release(g, ev[s][fh])
                    wl_war[j] = kb.tok("pe")
                    if j + 3 < nload:
                        load_w2(j + 3)
                prev_qof_o_reader = kb.tok("pe")
                for s in range(4):
                    sp = s % 2
                    last_store[sp] = kb.dma(
                        "scalar", outr[:, 4 * i + s, :], xbuf[:, par, s, :],
                        deps=ev[s] + [last_store[sp]], sem="so%d" % sp)
                x_free[i] = (last_store[0], last_store[1])
                if i + 2 < N1:
                    load_x2(i + 2)

            kb.op("sync", lambda E: E.nop(),
                  deps=[last_store[0], last_store[1]])
            kb.flush()

        with ExitStack() as p3:
            e3 = p3.enter_context
            acc = e3(nc.sbuf_tensor("p3_acc", [128, S3, D], F32))
            xn_sb = e3(nc.sbuf_tensor("p3_xn", [128, KC, P3T], BF16))
            wd_sb = e3(nc.sbuf_tensor("p3_wd", [128, HC * D], BF16))
            h_sb = e3(nc.sbuf_tensor("p3_h", [128, HC, P3T], BF16))
            sgc = e3(nc.sbuf_tensor("p3_sgc", [128, 2, P3T], BF16))
            wgu = e3(nc.sbuf_tensor("p3_wgu", [128, 4, 2, KC, 128], BF16))
            ppa = e3(nc.psum_tensor("p3_ppa", [128, 2048], F32))
            ppb = e3(nc.psum_tensor("p3_ppb", [128, 2048], F32))
            pool = PPool([(ppa, 4), (ppb, 4)])

            wge_r = [wge_d.ap()[e].rearrange("(k p) h -> p k h", p=128)
                     for e in range(NE)]
            wue_r = [wue_d.ap()[e].rearrange("(k p) h -> p k h", p=128)
                     for e in range(NE)]
            wde_r = [wde_d.ap()[e].rearrange("(c p) d -> p c d", p=128)
                     for e in range(NE)]
            shwg_r = shwg_d.ap().rearrange("(k p) h -> p k h", p=128)
            shwu_r = shwu_d.ap().rearrange("(k p) h -> p k h", p=128)
            shwd_r = shwd_d.ap().rearrange("(c p) d -> p c d", p=128)

            def g_src(e, hc):
                if e < NE:
                    return wge_r[e][:, :, hc * 128:(hc + 1) * 128]
                off = (e - NE) * H + hc * 128
                return shwg_r[:, :, off:off + 128]

            def u_src(e, hc):
                if e < NE:
                    return wue_r[e][:, :, hc * 128:(hc + 1) * 128]
                off = (e - NE) * H + hc * 128
                return shwu_r[:, :, off:off + 128]

            def d_src(e):
                if e < NE:
                    return wde_r[e]
                return shwd_r[:, (e - NE) * HC:(e - NE + 1) * HC, :]

            NEX = NE + 2

            def load_xn3(q, dep):
                t = None
                for j in range(NT3):
                    t = kb.dma("sync",
                               xn_sb[:, :, j * 512:(j + 1) * 512],
                               xn_d.ap()[NT3 * q + j].rearrange("k p t -> p k t"),
                               deps=[dep] if j == 0 else [], sem="sxl")
                return t

            acc_store = None
            prev_dn_pe = None
            prev_g_pe = [None] * 4
            prev_h = [None, None]
            prev_stt = {}
            xn_next = load_xn3(0, None)
            for q in range(N3):
                xn_ready = xn_next
                acc_ld = kb.dma("sync", acc[:, :, :],
                                outr[:, q * S3:(q + 1) * S3, :],
                                deps=[acc_store], sem="sacc")

                for e in range(NEX):
                    gu_tok = [None] * HC
                    wd_tok = [None] * HC

                    def load_gu(hc, e=e, gu_tok=gu_tok):
                        par = hc % 4
                        dep = [prev_g_pe[par]]
                        kb.dma("gpsimd", wgu[:, par, 0, :, :], g_src(e, hc),
                               deps=dep, sem="sw%d" % par)
                        gu_tok[hc] = kb.dma("gpsimd", wgu[:, par, 1, :, :],
                                            u_src(e, hc), sem="sw%d" % par)

                    def load_wd(hc, e=e, wd_tok=wd_tok):
                        dep = [prev_dn_pe] if hc == 0 else []
                        wd_tok[hc] = kb.dma(
                            "gpsimd",
                            wd_sb[:, hc * D:(hc + 1) * D],
                            d_src(e)[:, hc, :],
                            deps=dep, sem="swc")

                    for hc in range(min(4, HC)):
                        load_gu(hc)
                    load_wd(0)
                    h_tok = None
                    for hc in range(HC):
                        par = hc % 4
                        sp = hc % 2
                        g = pool.claim(NT3)
                        mm = None
                        for n in range(NT3):
                            for kc in range(KC):
                                dep = ([gu_tok[hc], xn_ready, prev_h[sp]] + g[4]) \
                                    if (kc == 0 and n == 0) else []
                                mm = kb.op("tensor", lambda E, kc=kc, n=n, g=g, par=par: E.matmul(
                                    pool.ap(g, n * 512, (n + 1) * 512),
                                    wgu[:, par, 0, kc, :],
                                    xn_sb[:, kc, n * 512:(n + 1) * 512],
                                    start=(kc == 0), stop=(kc == KC - 1)),
                                    deps=dep,
                                    inc="pe" if (kc == KC - 1 and n == NT3 - 1) else None)
                        sil = kb.op("scalar", lambda E, sp=sp, g=g: E.activation(
                            sgc[:, sp, :], pool.ap(g), AF.Silu),
                            deps=[mm, prev_h[sp]], inc="a")
                        pool.release(g, sil)
                        g2 = pool.claim(NT3)
                        mm = None
                        for n in range(NT3):
                            for kc in range(KC):
                                dep = ([gu_tok[hc]] + g2[4]) if (kc == 0 and n == 0) else []
                                mm = kb.op("tensor", lambda E, kc=kc, n=n, g=g2, par=par: E.matmul(
                                    pool.ap(g, n * 512, (n + 1) * 512),
                                    wgu[:, par, 1, kc, :],
                                    xn_sb[:, kc, n * 512:(n + 1) * 512],
                                    start=(kc == 0), stop=(kc == KC - 1)),
                                    deps=dep,
                                    inc="pe" if (kc == KC - 1 and n == NT3 - 1) else None)
                        prev_g_pe[par] = kb.tok("pe")
                        if hc + 4 < HC:
                            load_gu(hc + 4)
                        if hc + 1 < HC:
                            load_wd(hc + 1)
                        h_tok = kb.op("vector", lambda E, hc=hc, g=g2, sp=sp: E.tensor_tensor(
                            h_sb[:, hc, :], pool.ap(g), sgc[:, sp, :],
                            ALU.mult), deps=[mm, sil], inc="v")
                        pool.release(g2, h_tok)
                        prev_h[sp] = h_tok
                        if e == NEX - 1 and hc == HC - 1 and q + 1 < N3:
                            xn_next = load_xn3(q + 1, kb.tok("pe"))
                    for s in range(S3):
                        sub = q * S3 + s
                        if e < NE:
                            c_scal = c_all[:, sub, e:e + 1]
                        else:
                            c_scal = sg_all[:, sub:sub + 1]
                        for fp in range(2):
                            g = pool.claim(2)
                            mm = None
                            for fh in range(2):
                                for hc in range(HC):
                                    dep = ([h_tok, wd_tok[HC - 1], acc_ld] + g[4]) \
                                        if (hc == 0 and fh == 0) else []
                                    mm = kb.op("tensor", lambda E, hc=hc, fh=fh, s=s, fp=fp, g=g: E.matmul(
                                        pool.ap(g, fh * 512, (fh + 1) * 512),
                                        h_sb[:, hc, s * 128:(s + 1) * 128],
                                        wd_sb[:, hc * D + fp * 1024 + fh * 512:
                                              hc * D + fp * 1024 + (fh + 1) * 512],
                                        start=(hc == 0), stop=(hc == HC - 1)),
                                        deps=dep,
                                        inc="pe" if (hc == HC - 1 and fh == 1) else None)
                            stt = kb.op("vector", lambda E, s=s, fp=fp, g=g, c_scal=c_scal: E.scalar_tensor_tensor(
                                acc[:, s, fp * 1024:(fp + 1) * 1024],
                                pool.ap(g), c_scal,
                                acc[:, s, fp * 1024:(fp + 1) * 1024],
                                ALU.mult, ALU.add),
                                deps=[mm, acc_ld, prev_stt.get((s, fp))], inc="v")
                            prev_stt[(s, fp)] = stt
                            pool.release(g, stt)
                    prev_dn_pe = kb.tok("pe")
                st_deps = [prev_stt[(s, fp)] for s in range(S3) for fp in range(2)]
                acc_store = kb.dma("scalar", outr[:, q * S3:(q + 1) * S3, :],
                                   acc[:, :, :], deps=st_deps + [acc_store],
                                   sem="sacc")
            kb.op("sync", lambda E: E.nop(), deps=[acc_store])
            kb.flush()

    nc.finalize()
    return nc


def prep_inputs(L0_attn_wq, L0_attn_wk, L0_attn_wv, L0_attn_wu, L0_attn_wo,
                L0_ffn_gate_w, L0_ffn_w_gate, L0_ffn_w_up, L0_ffn_w_down,
                L0_ffn_shgate_w, L0_ffn_shw_gate, L0_ffn_shw_up,
                L0_ffn_shw_down):
    f32c = lambda a: np.ascontiguousarray(np.asarray(a, dtype=np.float32))
    wqT = f32c(np.asarray(L0_attn_wq).T)
    wkT = f32c(np.asarray(L0_attn_wk).T)
    wvT = f32c(np.asarray(L0_attn_wv).T)
    wuT = f32c(np.asarray(L0_attn_wu).T)
    woT = f32c(np.asarray(L0_attn_wo).T)
    gwT = np.zeros((D, 16), dtype=np.float32)
    gwT[:, 0:8] = np.asarray(L0_ffn_gate_w, dtype=np.float32).T
    gwT[:, 8] = np.asarray(L0_ffn_shgate_w, dtype=np.float32)
    wge = f32c(L0_ffn_w_gate)
    wue = f32c(L0_ffn_w_up)
    wde = f32c(L0_ffn_w_down)
    shwg = f32c(L0_ffn_shw_gate)
    shwu = f32c(L0_ffn_shw_up)
    shwd = f32c(L0_ffn_shw_down)
    identw = np.eye(128, dtype=np.float32)
    return dict(wqT=wqT, wkT=wkT, wvT=wvT, wuT=wuT, woT=woT, gwT=gwT,
                wge=wge, wue=wue, wde=wde, shwg=shwg, shwu=shwu, shwd=shwd,
                identw=identw)


def kernel(L0_attn_x, L0_attn_wq, L0_attn_wk, L0_attn_wv, L0_attn_wu,
           L0_attn_wo, L0_ffn_gate_w, L0_ffn_w_gate, L0_ffn_w_up,
           L0_ffn_w_down, L0_ffn_shgate_w, L0_ffn_shw_gate, L0_ffn_shw_up,
           L0_ffn_shw_down):
    x = np.asarray(L0_attn_x, dtype=np.float32)
    B, L, _ = x.shape
    ncores = 8
    T = B * L // ncores

    common = prep_inputs(L0_attn_wq, L0_attn_wk, L0_attn_wv, L0_attn_wu,
                         L0_attn_wo, L0_ffn_gate_w, L0_ffn_w_gate,
                         L0_ffn_w_up, L0_ffn_w_down, L0_ffn_shgate_w,
                         L0_ffn_shw_gate, L0_ffn_shw_up, L0_ffn_shw_down)
    halves_per_seq = L // T
    groups = [[s * halves_per_seq + j for j in range(halves_per_seq)]
              for s in range(B)]
    nc = build_program(T, groups)

    xf = x.reshape(ncores, T, D)
    in_maps = [dict(common, x=np.ascontiguousarray(xf[c]))
               for c in range(ncores)]
    res = run_bass_kernel_spmd(nc, in_maps, core_ids=list(range(ncores)))
    out = np.stack([res.results[c]["out"] for c in range(ncores)], axis=0)
    return out.reshape(B, L, D)


# version 11
# speedup vs baseline: 1.0403x; 1.0403x over previous
import numpy as np
from contextlib import ExitStack

import concourse.bass as bass
import concourse.mybir as mybir
from concourse.bass_utils import run_bass_kernel_spmd

F32 = mybir.dt.float32
BF16 = mybir.dt.bfloat16
AF = mybir.ActivationFunctionType
ALU = mybir.AluOpType
AX = mybir.AxisListType

D = 2048
KC = D // 128
HEADS = 16
HD = 128
NE = 8
H = 1408
HC = H // 128
HS = 2816
RSCALE = float(1.0 / np.sqrt(HD))
SRMS2 = 1.0 / HD
LN_EPS = 1e-5

ENGINES = ("tensor", "vector", "scalar", "gpsimd", "sync")


class KB:

    ENGINE_SEMS = ("pe", "v", "a")

    def __init__(self, nc, sems):
        self.nc = nc
        self.sems = sems
        self.cnt = {k: 0 for k in sems}
        self.ops = {e: [] for e in ENGINES}
        self.waited = {e: {} for e in ENGINES}
        self.max_waited = {}

    def op(self, eng, fn, deps=(), inc=None, step=1):
        lst = self.ops[eng]
        w = self.waited[eng]
        for d in deps:
            if d is None:
                continue
            s, v = d
            assert v <= self.cnt[s], (s, v, self.cnt[s])
            if w.get(s, 0) < v:
                sem = self.sems[s]
                lst.append(lambda E, sem=sem, v=v: E.wait_ge(sem, v))
                w[s] = v
                if v > self.max_waited.get(s, 0):
                    self.max_waited[s] = v
        if inc is None:
            lst.append(fn)
            return None
        self.cnt[inc] += step
        tok = (inc, self.cnt[inc])
        sem = self.sems[inc]
        lst.append(lambda E, fn=fn, sem=sem, step=step: fn(E).then_inc(sem, step))
        return tok

    def dma(self, eng, out, in_, deps=(), sem=None):
        return self.op(eng, lambda E: E.dma_start(out=out, in_=in_), deps,
                       inc=sem, step=16)

    def tok(self, s):
        return (s, self.cnt[s])

    def flush(self, skip=()):
        for s in self.cnt:
            if s not in self.ENGINE_SEMS and s not in skip and \
                    self.cnt[s] > self.max_waited.get(s, 0):
                self.op("sync", lambda E: E.nop(), deps=[(s, self.cnt[s])])
        ops = self.ops
        with self.nc.Block() as blk:
            for e in ENGINES:
                if ops[e]:
                    def runner(E, seq=tuple(ops[e])):
                        for f in seq:
                            f(E)
                    getattr(blk, e)(runner)
        self.ops = {e: [] for e in ENGINES}
        self.waited = {e: dict(self.cnt) for e in ENGINES}
        self.max_waited = dict(self.cnt)


class PPool:

    def __init__(self, tensors):
        self.t = tensors
        self.cur = [0] * len(tensors)
        self.ctok = [[None] * n for (_, n) in tensors]
        self.rr = 0

    def claim(self, n):
        for _ in range(len(self.t)):
            ti = self.rr % len(self.t)
            self.rr += 1
            handle, nslots = self.t[ti]
            if nslots < n:
                continue
            if self.cur[ti] + n > nslots:
                self.cur[ti] = 0
            off = self.cur[ti]
            self.cur[ti] += n
            deps = [self.ctok[ti][off + j] for j in range(n)]
            return (ti, off, n, handle, deps)
        raise RuntimeError("no psum tensor big enough")

    def ap(self, grp, c0=0, c1=None):
        ti, off, n, handle, _ = grp
        base = off * 512
        if c1 is None:
            c1 = n * 512
        return handle[:, base + c0: base + c1]

    def release(self, grp, tok):
        ti, off, n, _, _ = grp
        for j in range(n):
            self.ctok[ti][off + j] = tok


STAT_NAMES = ("s1n", "s2", "mean_n", "msq", "ep", "lnv", "rstd", "nmr")


def _ln_xn(kb, xin_3d, xout_3d, nsubs, sq_scr, stats, x_tok, prev, war_out, cst):
    t_red = kb.op("vector", lambda E: E.reduce_sum(
        stats["s1n"], xin_3d, axis=AX.X, negate=True),
        deps=[x_tok, prev], inc="v")
    sq_tok = None
    for s in range(nsubs):
        sq_tok = kb.op("scalar", lambda E, s=s: E.activation(
            sq_scr[:, :], xin_3d[:, s, :], AF.Square,
            accum_out=stats["s2"][:, s: s + 1]),
            deps=[x_tok, prev] if s == 0 else [], inc="a")
    t1 = kb.op("vector", lambda E: E.tensor_scalar(
        stats["mean_n"], stats["s1n"], 1.0 / D, None, ALU.mult),
        deps=[t_red], inc="v")
    t2 = kb.op("vector", lambda E: E.tensor_tensor(
        stats["msq"], stats["mean_n"], stats["mean_n"], ALU.mult),
        deps=[t1], inc="v")
    t3 = kb.op("vector", lambda E: E.tensor_scalar(
        stats["ep"], stats["s2"], 1.0 / D, None, ALU.mult),
        deps=[sq_tok], inc="v")
    t4 = kb.op("vector", lambda E: E.tensor_tensor(
        stats["ep"], stats["ep"], stats["msq"], ALU.subtract),
        deps=[t2, t3], inc="v")
    a1 = kb.op("scalar", lambda E: E.activation(
        stats["lnv"], stats["ep"], AF.Ln, bias=cst["eps"]), deps=[t4], inc="a")
    a2 = kb.op("scalar", lambda E: E.activation(
        stats["rstd"], stats["lnv"], AF.Exp, scale=cst["mhalf"]), deps=[a1], inc="a")
    t5 = kb.op("vector", lambda E: E.tensor_tensor(
        stats["nmr"], stats["mean_n"], stats["rstd"], ALU.mult),
        deps=[a2], inc="v")
    xn_tok = None
    for s in range(nsubs):
        xn_tok = kb.op("vector", lambda E, s=s: E.tensor_scalar(
            xout_3d[:, s, :], xin_3d[:, s, :],
            stats["rstd"][:, s: s + 1], stats["nmr"][:, s: s + 1],
            ALU.mult, ALU.add),
            deps=([t5, sq_tok] + list(war_out)) if s == 0 else [], inc="v")
    return xn_tok


def build_program(T, groups):
    nc = bass.Bass("TRN2", target_bir_lowering=False)
    NSUB = T // 128
    N1 = max(1, T // 512)
    P3T = 1024 if T % 1024 == 0 else 512
    N3 = T // P3T
    S3 = P3T // 128
    NT3 = P3T // 512

    x_d = nc.dram_tensor("x", [T, D], F32, kind="ExternalInput")
    wqT_d = nc.dram_tensor("wqT", [D, D], F32, kind="ExternalInput")
    wkT_d = nc.dram_tensor("wkT", [D, D], F32, kind="ExternalInput")
    wvT_d = nc.dram_tensor("wvT", [D, D], F32, kind="ExternalInput")
    wuT_d = nc.dram_tensor("wuT", [D, D], F32, kind="ExternalInput")
    woT_d = nc.dram_tensor("woT", [D, D], F32, kind="ExternalInput")
    gwT_d = nc.dram_tensor("gwT", [D, 16], F32, kind="ExternalInput")
    wge_d = nc.dram_tensor("wge", [NE, D, H], F32, kind="ExternalInput")
    wue_d = nc.dram_tensor("wue", [NE, D, H], F32, kind="ExternalInput")
    wde_d = nc.dram_tensor("wde", [NE, H, D], F32, kind="ExternalInput")
    shwg_d = nc.dram_tensor("shwg", [D, HS], F32, kind="ExternalInput")
    shwu_d = nc.dram_tensor("shwu", [D, HS], F32, kind="ExternalInput")
    shwd_d = nc.dram_tensor("shwd", [HS, D], F32, kind="ExternalInput")
    id_d = nc.dram_tensor("identw", [128, 128], F32, kind="ExternalInput")
    out_d = nc.dram_tensor("out", [T, D], F32, kind="ExternalOutput")
    kvp_d = nc.dram_tensor("kv_part", [128, D], F32, kind="Internal")
    kvf_d = nc.dram_tensor("kv_full", [128, D], F32, kind="Internal")
    xn_d = nc.dram_tensor("xn_sav", [N1, KC, 128, 512], BF16, kind="Internal")

    xr = x_d.ap().rearrange("(a p) d -> p a d", p=128)
    outr = out_d.ap().rearrange("(a p) d -> p a d", p=128)

    with ExitStack() as top:
        ec = top.enter_context
        sems = {}
        for s in ("pe", "v", "a", "scc", "swc",
                  "sx0", "sx1", "sw0", "sw1", "sw2", "sw3",
                  "so0", "so1", "sacc", "skv", "sxn", "sxl"):
            sems[s] = ec(nc.semaphore(s))
        kb = KB(nc, sems)

        ident = ec(nc.sbuf_tensor("ident", [128, 128], F32))
        c_all = ec(nc.sbuf_tensor("c_all", [128, NSUB, NE], F32))
        sg_all = ec(nc.sbuf_tensor("sg_all", [128, NSUB], F32))
        gw_sb = ec(nc.sbuf_tensor("gw_sb", [128, KC, 16], BF16))
        cst_t = ec(nc.sbuf_tensor("cst", [128, 8], F32))
        cst = {n: cst_t[:, i:i + 1] for i, n in enumerate(
            ("eps", "tiny", "mhalf", "srms2", "mone"))}

        with ExitStack() as p1:
            e1 = p1.enter_context
            xbuf = e1(nc.sbuf_tensor("p1_x", [128, 2, 4, D], F32))
            sq_scr = e1(nc.sbuf_tensor("p1_sq", [128, D], BF16))
            st_t = e1(nc.sbuf_tensor("p1_st", [128, 8, 4], F32))
            stt = {n: st_t[:, i, :] for i, n in enumerate(STAT_NAMES)}
            xn_sb = e1(nc.sbuf_tensor("p1_xn", [128, KC, 512], BF16))
            k_sb = e1(nc.sbuf_tensor("p1_k", [128, 4, D], BF16))
            v_sb = e1(nc.sbuf_tensor("p1_v", [128, 4, D], BF16))
            wbuf = e1(nc.sbuf_tensor("p1_w", [128, 2, 2, KC, 512], BF16))
            kvs = e1(nc.sbuf_tensor("p1_kvs", [128, D], F32))
            ppa = e1(nc.psum_tensor("p1_ppa", [128, 2048], F32))
            ppb = e1(nc.psum_tensor("p1_ppb", [128, 2048], F32))
            pool = PPool([(ppa, 4), (ppb, 4)])

            kv_zero = kb.op("vector", lambda E: E.memset(kvs[:, :], 0.0),
                            inc="v")
            cst_tok = None
            for ci, val in enumerate((LN_EPS, 1e-30, -0.5, SRMS2, -1.0)):
                cst_tok = kb.op("vector", lambda E, ci=ci, val=val: E.memset(
                    cst_t[:, ci:ci + 1], val), inc="v")
            kb.dma("gpsimd", ident[:, :], id_d[:, :], sem="swc")
            gwr = gwT_d.ap().rearrange("(k p) e -> p k e", p=128)
            gw_tok = kb.dma("gpsimd", gw_sb[:, :, :], gwr, sem="swc")
            ident_tok = gw_tok

            wkr = wkT_d.ap().rearrange("(k p) f -> p k f", p=128)
            wvr = wvT_d.ap().rearrange("(k p) f -> p k f", p=128)

            NFG = 4
            nload = N1 * NFG
            wl_tok = [None] * nload
            wl_war = [None] * nload

            def load_w1(j):
                fg = j % NFG
                par = j % 2
                dep = [wl_war[j - 2]] if j >= 2 else []
                kb.dma("gpsimd", wbuf[:, par, 0, :, :],
                       wkr[:, :, fg * 512:(fg + 1) * 512], deps=dep,
                       sem="sw%d" % par)
                wl_tok[j] = kb.dma("gpsimd", wbuf[:, par, 1, :, :],
                                   wvr[:, :, fg * 512:(fg + 1) * 512],
                                   sem="sw%d" % par)

            x_tok = [None] * N1
            x_free = [None] * N1

            def load_x1(i):
                dep = [x_free[i - 2]] if i >= 2 else []
                x_tok[i] = kb.dma("sync", xbuf[:, i % 2, :, :],
                                  xr[:, 4 * i:4 * i + 4, :], deps=dep,
                                  sem="sx%d" % (i % 2))

            load_x1(0)
            if N1 > 1:
                load_x1(1)
            load_w1(0)
            if nload > 1:
                load_w1(1)

            ln_tok = [None] * (N1 + 1)
            ln_tok[0] = _ln_xn(kb, xbuf[:, 0, :, :], xbuf[:, 0, :, :], 4,
                               sq_scr, stt, x_tok[0], cst_tok, [], cst)

            xn_store = None
            prev_xn_reader = None
            prev_kv_done = None
            kv_acc = [kv_zero] * 4
            for i in range(N1):
                par = i % 2
                xnf_tok = None
                for kc in range(KC):
                    g = pool.claim(1)
                    tr = None
                    for s in range(4):
                        dep = ([ln_tok[i], ident_tok] + g[4]) if s == 0 else []
                        tr = kb.op("tensor", lambda E, kc=kc, s=s, g=g, par=par: E.transpose(
                            pool.ap(g, s * 128, (s + 1) * 128),
                            xbuf[:, par, s, kc * 128:(kc + 1) * 128],
                            ident[:, :]), deps=dep,
                            inc="pe" if s == 3 else None)
                    xnf_tok = kb.op("scalar", lambda E, kc=kc, g=g: E.activation(
                        xn_sb[:, kc, :], pool.ap(g), AF.Copy),
                        deps=[tr, prev_xn_reader, xn_store], inc="a")
                    pool.release(g, xnf_tok)
                x_free[i] = kb.tok("pe")
                if i + 2 < N1:
                    load_x1(i + 2)
                xn_store = kb.dma(
                    "sync", xn_d.ap()[i].rearrange("k p t -> p k t"),
                    xn_sb[:, :, :], deps=[xnf_tok], sem="sxn")

                if i + 1 < N1:
                    ln_tok[i + 1] = _ln_xn(
                        kb, xbuf[:, (i + 1) % 2, :, :],
                        xbuf[:, (i + 1) % 2, :, :], 4,
                        sq_scr, stt, x_tok[i + 1], ln_tok[i], [], cst)

                evac_tok = None
                for fg in range(NFG):
                    j = i * NFG + fg
                    for m in range(2):
                        for s in range(4):
                            g = pool.claim(1)
                            mm = None
                            for kc in range(KC):
                                dep = ([wl_tok[j], xnf_tok] + g[4]) if kc == 0 else []
                                mm = kb.op("tensor", lambda E, kc=kc, s=s, m=m, g=g, par2=j % 2: E.matmul(
                                    pool.ap(g),
                                    xn_sb[:, kc, s * 128:(s + 1) * 128],
                                    wbuf[:, par2, m, kc, :],
                                    start=(kc == 0), stop=(kc == KC - 1)),
                                    deps=dep, inc="pe" if kc == KC - 1 else None)
                            if m == 0:
                                evac_tok = kb.op("vector", lambda E, s=s, fg=fg, g=g: E.tensor_scalar(
                                    k_sb[:, s, fg * 512:(fg + 1) * 512],
                                    pool.ap(g), 0.0, RSCALE, ALU.max, ALU.mult),
                                    deps=[mm, prev_kv_done], inc="v")
                            else:
                                evac_tok = kb.op("vector", lambda E, s=s, fg=fg, g=g: E.tensor_copy(
                                    v_sb[:, s, fg * 512:(fg + 1) * 512],
                                    pool.ap(g)),
                                    deps=[mm, prev_kv_done], inc="v")
                            pool.release(g, evac_tok)
                    wl_war[j] = kb.tok("pe")
                    if j + 2 < nload:
                        load_w1(j + 2)
                prev_xn_reader = kb.tok("pe")

                for hg in range(4):
                    g = pool.claim(1)
                    mm = None
                    for hq in range(4):
                        h = hg * 4 + hq
                        for s in range(4):
                            dep = ([evac_tok] + g[4]) if (hq == 0 and s == 0) else []
                            mm = kb.op("tensor", lambda E, s=s, h=h, hq=hq, g=g: E.matmul(
                                pool.ap(g, hq * 128, (hq + 1) * 128),
                                k_sb[:, s, h * 128:(h + 1) * 128],
                                v_sb[:, s, h * 128:(h + 1) * 128],
                                start=(s == 0), stop=(s == 3)),
                                deps=dep,
                                inc="pe" if (hq == 3 and s == 3) else None)
                    kv_acc[hg] = kb.op("vector", lambda E, hg=hg, g=g: E.tensor_tensor(
                        kvs[:, hg * 512:(hg + 1) * 512],
                        kvs[:, hg * 512:(hg + 1) * 512],
                        pool.ap(g), ALU.add),
                        deps=[mm, kv_acc[hg]], inc="v")
                    pool.release(g, kv_acc[hg])
                prev_kv_done = kb.tok("pe")

            t2 = kb.dma("scalar", kvp_d[:, :], kvs[:, :],
                        deps=[kv_acc[0], kv_acc[1], kv_acc[2], kv_acc[3]],
                        sem="skv")
            t3 = kb.op("gpsimd", lambda E: E.collective_compute(
                "AllReduce", ALU.add, replica_groups=groups,
                ins=[kvp_d.ap().opt()], outs=[kvf_d.ap().opt()]),
                deps=[t2], inc="scc")
            kb.flush(skip=("scc",))

        with ExitStack() as p2:
            e2 = p2.enter_context
            xbuf = e2(nc.sbuf_tensor("p2_x", [128, 2, 4, D], F32))
            o_sb = e2(nc.sbuf_tensor("p2_osb", [128, 4, D], F32))
            kv_bf = e2(nc.sbuf_tensor("p2_kvb", [128, D], BF16))
            xn_sb = e2(nc.sbuf_tensor("p2_xn", [128, KC, 512], BF16))
            u_sb = e2(nc.sbuf_tensor("p2_u", [128, 4, D], BF16))
            qof = e2(nc.sbuf_tensor("p2_qof", [128, 16, 512], BF16))
            wsl = e2(nc.sbuf_tensor("p2_w", [128, 3, KC, 512], BF16))
            log_sb = e2(nc.sbuf_tensor("p2_log", [128, 4, 16], F32))
            gsc = e2(nc.sbuf_tensor("p2_gsc", [128, 4, 32], F32))
            srms_scr = e2(nc.sbuf_tensor("p2_ss", [128, D], BF16))
            ssq = e2(nc.sbuf_tensor("p2_ssq", [128, 16], F32))
            rn = e2(nc.sbuf_tensor("p2_rn", [128, 16], F32))
            ppa = e2(nc.psum_tensor("p2_ppa", [128, 2048], F32))
            ppb = e2(nc.psum_tensor("p2_ppb", [128, 2048], F32))
            pool = PPool([(ppa, 4), (ppb, 4)])

            wqr = wqT_d.ap().rearrange("(k p) f -> p k f", p=128)
            wur = wuT_d.ap().rearrange("(k p) f -> p k f", p=128)
            wor = woT_d.ap().rearrange("(k p) f -> p k f", p=128)

            NSL = 12
            nload = N1 * NSL
            wl_tok = [None] * nload
            wl_war = [None] * nload

            def load_w2(j):
                sl = j % NSL
                par = j % 3
                src = (wqr, wur, wor)[sl // 4]
                fg = sl % 4
                dep = [wl_war[j - 3]] if j >= 3 else []
                wl_tok[j] = kb.dma("gpsimd", wsl[:, par, :, :],
                                   src[:, :, fg * 512:(fg + 1) * 512],
                                   deps=dep, sem="sw%d" % par)

            x_tok = [None] * N1
            x_free = [None] * N1

            def load_x2(i):
                dep = list(x_free[i - 2]) if i >= 2 else []
                x_tok[i] = kb.dma("sync", xbuf[:, i % 2, :, :],
                                  xr[:, 4 * i:4 * i + 4, :], deps=dep,
                                  sem="sx%d" % (i % 2))

            kv_rdy = None

            load_x2(0)
            if N1 > 1:
                load_x2(1)
            for j in range(3):
                load_w2(j)

            xn_ld = [None] * N1

            def load_xn2(i):
                dep = [prev_xn_reader]
                if i == 0:
                    dep.append(kb.tok("sxn"))
                xn_ld[i] = kb.dma("sync", xn_sb[:, :, :],
                                  xn_d.ap()[i].rearrange("k p t -> p k t"),
                                  deps=dep, sem="sxl")

            prev_xn_reader = None
            prev_gate = None
            prev_u_reader = None
            prev_qof_o_reader = None
            prev_obf_reader = None
            last_store = [None, None]
            load_xn2(0)
            for i in range(N1):
                par = i % 2
                xnf_tok = xn_ld[i]

                lg_evac = None
                for s in range(4):
                    g = pool.claim(1)
                    mm = None
                    for kc in range(KC):
                        dep = ([xnf_tok, gw_tok] + g[4]) if kc == 0 else []
                        mm = kb.op("tensor", lambda E, kc=kc, s=s, g=g: E.matmul(
                            pool.ap(g, 0, 16),
                            xn_sb[:, kc, s * 128:(s + 1) * 128],
                            gw_sb[:, kc, :],
                            start=(kc == 0), stop=(kc == KC - 1)),
                            deps=dep, inc="pe" if kc == KC - 1 else None)
                    lg_evac = kb.op("vector", lambda E, s=s, g=g: E.tensor_copy(
                        log_sb[:, s, :], pool.ap(g, 0, 16)),
                        deps=[mm, prev_gate], inc="v")
                    pool.release(g, lg_evac)
                e_v = gsc[:, :, 0:8]
                im_v = gsc[:, :, 8:16]
                e2_v = gsc[:, :, 16:24]
                m1_v = gsc[:, :, 24:25].squeeze(2)
                m2_v = gsc[:, :, 25:26].squeeze(2)
                den_v = gsc[:, :, 26:27].squeeze(2)
                ez_v = gsc[:, :, 27:28].squeeze(2)
                g1 = kb.op("scalar", lambda E: E.activation(
                    e_v, log_sb[:, :, 0:8], AF.Exp), deps=[lg_evac], inc="a")
                g2 = kb.op("vector", lambda E: E.reduce_max(
                    m1_v, e_v, axis=AX.X), deps=[g1], inc="v")
                g3 = kb.op("vector", lambda E: E.tensor_tensor(
                    im_v, e_v, m1_v.unsqueeze(2).broadcast_to([128, 4, 8]),
                    ALU.is_ge), deps=[g2], inc="v")
                g4 = kb.op("vector", lambda E: E.tensor_scalar(
                    im_v, im_v, -1.0, 1.0, ALU.mult, ALU.add), deps=[g3], inc="v")
                g5 = kb.op("vector", lambda E: E.tensor_tensor(
                    e2_v, e_v, im_v, ALU.mult), deps=[g4], inc="v")
                g6 = kb.op("vector", lambda E: E.reduce_max(
                    m2_v, e2_v, axis=AX.X), deps=[g5], inc="v")
                g7 = kb.op("vector", lambda E: E.tensor_tensor(
                    im_v, e_v, m2_v.unsqueeze(2).broadcast_to([128, 4, 8]),
                    ALU.is_ge), deps=[g6], inc="v")
                g8 = kb.op("vector", lambda E: E.tensor_tensor(
                    den_v, m1_v, m2_v, ALU.add), deps=[g6], inc="v")
                g9 = kb.op("vector", lambda E: E.reciprocal(
                    den_v, den_v), deps=[g8], inc="v")
                g10 = kb.op("vector", lambda E: E.tensor_tensor(
                    e2_v, e_v, im_v, ALU.mult), deps=[g7], inc="v")
                g11 = kb.op("vector", lambda E, i=i: E.tensor_tensor(
                    c_all[:, 4 * i:4 * i + 4, :], e2_v,
                    den_v.unsqueeze(2).broadcast_to([128, 4, 8]), ALU.mult),
                    deps=[g9, g10], inc="v")
                sg1 = kb.op("scalar", lambda E: E.activation(
                    ez_v.unsqueeze(2), log_sb[:, :, 8:9], AF.Exp, scale=cst["mone"]),
                    deps=[lg_evac], inc="a")
                sg2 = kb.op("vector", lambda E: E.tensor_scalar(
                    ez_v, ez_v, 1.0, None, ALU.add), deps=[sg1], inc="v")
                sg3 = kb.op("vector", lambda E, i=i: E.reciprocal(
                    sg_all[:, 4 * i:4 * i + 4], ez_v), deps=[sg2], inc="v")
                prev_gate = sg3

                q_evac = None
                for hg in range(4):
                    j = i * NSL + hg
                    g = pool.claim(4)
                    mm = None
                    for hq in range(4):
                        for kc in range(KC):
                            dep = ([wl_tok[j], xnf_tok, prev_qof_o_reader] + g[4]) \
                                if (hq == 0 and kc == 0) else []
                            mm = kb.op("tensor", lambda E, kc=kc, hq=hq, g=g, par2=j % 3: E.matmul(
                                pool.ap(g, hq * 512, (hq + 1) * 512),
                                wsl[:, par2, kc, hq * 128:(hq + 1) * 128],
                                xn_sb[:, kc, :],
                                start=(kc == 0), stop=(kc == KC - 1)),
                                deps=dep,
                                inc="pe" if (hq == 3 and kc == KC - 1) else None)
                    q_evac = kb.op("vector", lambda E, hg=hg, g=g: E.tensor_scalar(
                        qof[:, 4 * hg:4 * hg + 4, :].rearrange("p a b -> p (a b)"),
                        pool.ap(g), 0.0, RSCALE, ALU.max, ALU.mult),
                        deps=[mm], inc="v")
                    pool.release(g, q_evac)
                    wl_war[j] = kb.tok("pe")
                    if j + 3 < nload:
                        load_w2(j + 3)

                u_tok = None

                def emit_U():
                    nonlocal prev_xn_reader, u_tok
                    for fg in range(4):
                        j = i * NSL + 4 + fg
                        for s in range(4):
                            g = pool.claim(1)
                            mm = None
                            for kc in range(KC):
                                dep = ([wl_tok[j], xnf_tok, prev_u_reader] + g[4]) \
                                    if kc == 0 else []
                                mm = kb.op("tensor", lambda E, kc=kc, s=s, g=g, par2=j % 3: E.matmul(
                                    pool.ap(g),
                                    xn_sb[:, kc, s * 128:(s + 1) * 128],
                                    wsl[:, par2, kc, :],
                                    start=(kc == 0), stop=(kc == KC - 1)),
                                    deps=dep, inc="pe" if kc == KC - 1 else None)
                            u_tok = kb.op("scalar", lambda E, s=s, fg=fg, g=g: E.activation(
                                u_sb[:, s, fg * 512:(fg + 1) * 512], pool.ap(g),
                                AF.Silu), deps=[mm], inc="a")
                            pool.release(g, u_tok)
                        wl_war[j] = kb.tok("pe")
                        if j + 3 < nload:
                            load_w2(j + 3)
                    prev_xn_reader = kb.tok("pe")
                    if i + 1 < N1:
                        load_xn2(i + 1)

                def emit_O(fuse_mult):
                    toks = [None] * 4
                    prev_t1 = None
                    for s in range(4):
                        g = pool.claim(4)
                        mm = None
                        for h in range(HEADS):
                            dep = ([q_evac, kv_rdy] + g[4]) if h == 0 else []
                            mm = kb.op("tensor", lambda E, s=s, h=h, g=g: E.matmul(
                                pool.ap(g, h * 128, (h + 1) * 128),
                                qof[:, h, s * 128:(s + 1) * 128],
                                kv_bf[:, h * 128:(h + 1) * 128],
                                start=True, stop=True),
                                deps=dep, inc="pe" if h == HEADS - 1 else None)
                        o_cp = kb.op("vector", lambda E, s=s, g=g: E.tensor_copy(
                            o_sb[:, s, :], pool.ap(g)),
                            deps=[mm, prev_obf_reader, prev_u_reader], inc="v")
                        pool.release(g, o_cp)
                        o_sq = kb.op("scalar", lambda E, s=s: E.activation(
                            srms_scr[:, :], o_sb[:, s, :], AF.Square),
                            deps=[o_cp, prev_t1], inc="a")
                        o_red = kb.op("vector", lambda E: E.reduce_sum(
                            ssq[:, :],
                            srms_scr[:, :].rearrange("p (h e) -> p h e", e=128),
                            axis=AX.X), deps=[o_sq], inc="v")
                        o_ln = kb.op("scalar", lambda E: E.activation(
                            rn[:, :], ssq[:, :], AF.Ln, bias=cst["tiny"], scale=cst["srms2"]),
                            deps=[o_red], inc="a")
                        o_ex = kb.op("scalar", lambda E: E.activation(
                            rn[:, :], rn[:, :], AF.Exp, scale=cst["mhalf"]),
                            deps=[o_ln], inc="a")
                        o_mn = kb.op("vector", lambda E: E.tensor_scalar(
                            rn[:, :], rn[:, :], 1e12, None, ALU.min),
                            deps=[o_ex], inc="v")
                        t1 = kb.op("vector", lambda E, s=s: E.tensor_tensor(
                            o_sb[:, s, :].rearrange("p (h e) -> p h e", e=128),
                            o_sb[:, s, :].rearrange("p (h e) -> p h e", e=128),
                            rn[:, :].unsqueeze(2).broadcast_to([128, 16, 128]),
                            ALU.mult), deps=[o_mn], inc="v")
                        prev_t1 = t1
                        if fuse_mult:
                            toks[s] = kb.op("vector", lambda E, s=s: E.tensor_tensor(
                                o_sb[:, s, :], o_sb[:, s, :], u_sb[:, s, :],
                                ALU.mult),
                                deps=[t1, u_tok if s == 0 else None], inc="v")
                        else:
                            toks[s] = t1
                    return toks

                if i == 0:
                    emit_U()
                    kv_rdy = kb.dma("gpsimd", kv_bf[:, :], kvf_d[:, :],
                                    deps=[t3], sem="skv")
                    op_tok = emit_O(fuse_mult=True)
                else:
                    t1_tok = emit_O(fuse_mult=False)
                    emit_U()
                    op_tok = [None] * 4
                    for s in range(4):
                        op_tok[s] = kb.op("vector", lambda E, s=s: E.tensor_tensor(
                            o_sb[:, s, :], o_sb[:, s, :], u_sb[:, s, :],
                            ALU.mult),
                            deps=[t1_tok[s], u_tok if s == 0 else None], inc="v")
                prev_u_reader = op_tok[3]

                of_a = [None] * 4
                of_v = [None] * 4
                for s in range(4):
                    for jj in range(4):
                        g = pool.claim(1)
                        tr = None
                        for q4 in range(4):
                            kc = jj * 4 + q4
                            dep = ([op_tok[s]] + g[4]) if q4 == 0 else []
                            tr = kb.op("tensor", lambda E, kc=kc, s=s, q4=q4, g=g: E.transpose(
                                pool.ap(g, q4 * 128, (q4 + 1) * 128),
                                o_sb[:, s, kc * 128:(kc + 1) * 128],
                                ident[:, :]), deps=dep,
                                inc="pe" if q4 == 3 else None)
                        if jj % 2 == 0:
                            of_a[s] = kb.op("scalar", lambda E, s=s, jj=jj, g=g: E.activation(
                                qof[:, 4 * jj:4 * jj + 4, s * 128:(s + 1) * 128],
                                pool.ap(g).rearrange("p (a b) -> p a b", b=128),
                                AF.Copy), deps=[tr], inc="a")
                            pool.release(g, of_a[s])
                        else:
                            of_v[s] = kb.op("vector", lambda E, s=s, jj=jj, g=g: E.tensor_copy(
                                qof[:, 4 * jj:4 * jj + 4, s * 128:(s + 1) * 128],
                                pool.ap(g).rearrange("p (a b) -> p a b", b=128)),
                                deps=[tr], inc="v")
                            pool.release(g, of_v[s])
                prev_obf_reader = kb.tok("pe")

                ev = [[None] * 4 for _ in range(4)]
                for fh in range(4):
                    j = i * NSL + 8 + fh
                    for s in range(4):
                        g = pool.claim(1)
                        mm = None
                        for kc in range(KC):
                            dep = ([wl_tok[j], of_a[s], of_v[s]] + g[4]) \
                                if kc == 0 else []
                            mm = kb.op("tensor", lambda E, kc=kc, s=s, g=g, par2=j % 3: E.matmul(
                                pool.ap(g),
                                qof[:, kc, s * 128:(s + 1) * 128],
                                wsl[:, par2, kc, :],
                                start=(kc == 0), stop=(kc == KC - 1)),
                                deps=dep, inc="pe" if kc == KC - 1 else None)
                        ev[s][fh] = kb.op("vector", lambda E, s=s, fh=fh, g=g, par=par: E.tensor_tensor(
                            xbuf[:, par, s, fh * 512:(fh + 1) * 512],
                            pool.ap(g),
                            xbuf[:, par, s, fh * 512:(fh + 1) * 512],
                            ALU.add),
                            deps=[mm, x_tok[i]], inc="v")
                        pool.release(g, ev[s][fh])
                    wl_war[j] = kb.tok("pe")
                    if j + 3 < nload:
                        load_w2(j + 3)
                prev_qof_o_reader = kb.tok("pe")
                for s in range(4):
                    sp = s % 2
                    last_store[sp] = kb.dma(
                        "scalar", outr[:, 4 * i + s, :], xbuf[:, par, s, :],
                        deps=ev[s] + [last_store[sp]], sem="so%d" % sp)
                x_free[i] = (last_store[0], last_store[1])
                if i + 2 < N1:
                    load_x2(i + 2)

            kb.op("sync", lambda E: E.nop(),
                  deps=[last_store[0], last_store[1]])
            kb.flush()

        with ExitStack() as p3:
            e3 = p3.enter_context
            acc = e3(nc.sbuf_tensor("p3_acc", [128, S3, D], F32))
            xn_sb = e3(nc.sbuf_tensor("p3_xn", [128, KC, P3T], BF16))
            wd_sb = e3(nc.sbuf_tensor("p3_wd", [128, HC * D], BF16))
            h_sb = e3(nc.sbuf_tensor("p3_h", [128, HC, P3T], BF16))
            sgc = e3(nc.sbuf_tensor("p3_sgc", [128, 2, P3T], BF16))
            wgu = e3(nc.sbuf_tensor("p3_wgu", [128, 4, 2, KC, 128], BF16))
            ppa = e3(nc.psum_tensor("p3_ppa", [128, 2048], F32))
            ppb = e3(nc.psum_tensor("p3_ppb", [128, 2048], F32))
            pool = PPool([(ppa, 4), (ppb, 4)])

            wge_r = [wge_d.ap()[e].rearrange("(k p) h -> p k h", p=128)
                     for e in range(NE)]
            wue_r = [wue_d.ap()[e].rearrange("(k p) h -> p k h", p=128)
                     for e in range(NE)]
            wde_r = [wde_d.ap()[e].rearrange("(c p) d -> p c d", p=128)
                     for e in range(NE)]
            shwg_r = shwg_d.ap().rearrange("(k p) h -> p k h", p=128)
            shwu_r = shwu_d.ap().rearrange("(k p) h -> p k h", p=128)
            shwd_r = shwd_d.ap().rearrange("(c p) d -> p c d", p=128)

            def g_src(e, hc):
                if e < NE:
                    return wge_r[e][:, :, hc * 128:(hc + 1) * 128]
                off = (e - NE) * H + hc * 128
                return shwg_r[:, :, off:off + 128]

            def u_src(e, hc):
                if e < NE:
                    return wue_r[e][:, :, hc * 128:(hc + 1) * 128]
                off = (e - NE) * H + hc * 128
                return shwu_r[:, :, off:off + 128]

            def d_src(e):
                if e < NE:
                    return wde_r[e]
                return shwd_r[:, (e - NE) * HC:(e - NE + 1) * HC, :]

            NEX = NE + 2

            def load_xn3(q, dep):
                t = None
                for j in range(NT3):
                    t = kb.dma("sync",
                               xn_sb[:, :, j * 512:(j + 1) * 512],
                               xn_d.ap()[NT3 * q + j].rearrange("k p t -> p k t"),
                               deps=[dep] if j == 0 else [], sem="sxl")
                return t

            acc_store = None
            prev_dn_pe = None
            prev_g_pe = [None] * 4
            prev_h = [None, None]
            prev_stt = {}
            xn_next = load_xn3(0, None)
            for q in range(N3):
                xn_ready = xn_next
                acc_ld = kb.dma("sync", acc[:, :, :],
                                outr[:, q * S3:(q + 1) * S3, :],
                                deps=[acc_store], sem="sacc")

                for e in range(NEX):
                    gu_tok = [None] * HC
                    wd_tok = [None] * HC

                    def load_gu(hc, e=e, gu_tok=gu_tok):
                        par = hc % 4
                        dep = [prev_g_pe[par]]
                        kb.dma("gpsimd", wgu[:, par, 0, :, :], g_src(e, hc),
                               deps=dep, sem="sw%d" % par)
                        gu_tok[hc] = kb.dma("gpsimd", wgu[:, par, 1, :, :],
                                            u_src(e, hc), sem="sw%d" % par)

                    def load_wd(hc, e=e, wd_tok=wd_tok):
                        dep = [prev_dn_pe] if hc == 0 else []
                        wd_tok[hc] = kb.dma(
                            "gpsimd",
                            wd_sb[:, hc * D:(hc + 1) * D],
                            d_src(e)[:, hc, :],
                            deps=dep, sem="swc")

                    for hc in range(min(4, HC)):
                        load_gu(hc)
                    load_wd(0)
                    h_tok = None
                    for hc in range(HC):
                        par = hc % 4
                        sp = hc % 2
                        g = pool.claim(NT3)
                        mm = None
                        for n in range(NT3):
                            for kc in range(KC):
                                dep = ([gu_tok[hc], xn_ready, prev_h[sp]] + g[4]) \
                                    if (kc == 0 and n == 0) else []
                                mm = kb.op("tensor", lambda E, kc=kc, n=n, g=g, par=par: E.matmul(
                                    pool.ap(g, n * 512, (n + 1) * 512),
                                    wgu[:, par, 0, kc, :],
                                    xn_sb[:, kc, n * 512:(n + 1) * 512],
                                    start=(kc == 0), stop=(kc == KC - 1)),
                                    deps=dep,
                                    inc="pe" if (kc == KC - 1 and n == NT3 - 1) else None)
                        sil = kb.op("scalar", lambda E, sp=sp, g=g: E.activation(
                            sgc[:, sp, :], pool.ap(g), AF.Silu),
                            deps=[mm, prev_h[sp]], inc="a")
                        pool.release(g, sil)
                        g2 = pool.claim(NT3)
                        mm = None
                        for n in range(NT3):
                            for kc in range(KC):
                                dep = ([gu_tok[hc]] + g2[4]) if (kc == 0 and n == 0) else []
                                mm = kb.op("tensor", lambda E, kc=kc, n=n, g=g2, par=par: E.matmul(
                                    pool.ap(g, n * 512, (n + 1) * 512),
                                    wgu[:, par, 1, kc, :],
                                    xn_sb[:, kc, n * 512:(n + 1) * 512],
                                    start=(kc == 0), stop=(kc == KC - 1)),
                                    deps=dep,
                                    inc="pe" if (kc == KC - 1 and n == NT3 - 1) else None)
                        prev_g_pe[par] = kb.tok("pe")
                        if hc + 4 < HC:
                            load_gu(hc + 4)
                        if hc + 1 < HC:
                            load_wd(hc + 1)
                        h_tok = kb.op("vector", lambda E, hc=hc, g=g2, sp=sp: E.tensor_tensor(
                            h_sb[:, hc, :], pool.ap(g), sgc[:, sp, :],
                            ALU.mult), deps=[mm, sil], inc="v")
                        pool.release(g2, h_tok)
                        prev_h[sp] = h_tok
                        if e == NEX - 1 and hc == HC - 1 and q + 1 < N3:
                            xn_next = load_xn3(q + 1, kb.tok("pe"))
                    for s in range(S3):
                        sub = q * S3 + s
                        if e < NE:
                            c_scal = c_all[:, sub, e:e + 1]
                        else:
                            c_scal = sg_all[:, sub:sub + 1]
                        for fp in range(2):
                            g = pool.claim(2)
                            mm = None
                            for fh in range(2):
                                for hc in range(HC):
                                    dep = ([h_tok, wd_tok[HC - 1], acc_ld] + g[4]) \
                                        if (hc == 0 and fh == 0) else []
                                    mm = kb.op("tensor", lambda E, hc=hc, fh=fh, s=s, fp=fp, g=g: E.matmul(
                                        pool.ap(g, fh * 512, (fh + 1) * 512),
                                        h_sb[:, hc, s * 128:(s + 1) * 128],
                                        wd_sb[:, hc * D + fp * 1024 + fh * 512:
                                              hc * D + fp * 1024 + (fh + 1) * 512],
                                        start=(hc == 0), stop=(hc == HC - 1)),
                                        deps=dep,
                                        inc="pe" if (hc == HC - 1 and fh == 1) else None)
                            stt = kb.op("vector", lambda E, s=s, fp=fp, g=g, c_scal=c_scal: E.scalar_tensor_tensor(
                                acc[:, s, fp * 1024:(fp + 1) * 1024],
                                pool.ap(g), c_scal,
                                acc[:, s, fp * 1024:(fp + 1) * 1024],
                                ALU.mult, ALU.add),
                                deps=[mm, acc_ld, prev_stt.get((s, fp))], inc="v")
                            prev_stt[(s, fp)] = stt
                            pool.release(g, stt)
                    prev_dn_pe = kb.tok("pe")
                st_deps = [prev_stt[(s, fp)] for s in range(S3) for fp in range(2)]
                acc_store = kb.dma("scalar", outr[:, q * S3:(q + 1) * S3, :],
                                   acc[:, :, :], deps=st_deps + [acc_store],
                                   sem="sacc")
            kb.op("sync", lambda E: E.nop(), deps=[acc_store])
            kb.flush()

    nc.finalize()
    return nc


def prep_inputs(L0_attn_wq, L0_attn_wk, L0_attn_wv, L0_attn_wu, L0_attn_wo,
                L0_ffn_gate_w, L0_ffn_w_gate, L0_ffn_w_up, L0_ffn_w_down,
                L0_ffn_shgate_w, L0_ffn_shw_gate, L0_ffn_shw_up,
                L0_ffn_shw_down):
    f32c = lambda a: np.ascontiguousarray(np.asarray(a, dtype=np.float32))
    wqT = f32c(np.asarray(L0_attn_wq).T)
    wkT = f32c(np.asarray(L0_attn_wk).T)
    wvT = f32c(np.asarray(L0_attn_wv).T)
    wuT = f32c(np.asarray(L0_attn_wu).T)
    woT = f32c(np.asarray(L0_attn_wo).T)
    gwT = np.zeros((D, 16), dtype=np.float32)
    gwT[:, 0:8] = np.asarray(L0_ffn_gate_w, dtype=np.float32).T
    gwT[:, 8] = np.asarray(L0_ffn_shgate_w, dtype=np.float32)
    wge = f32c(L0_ffn_w_gate)
    wue = f32c(L0_ffn_w_up)
    wde = f32c(L0_ffn_w_down)
    shwg = f32c(L0_ffn_shw_gate)
    shwu = f32c(L0_ffn_shw_up)
    shwd = f32c(L0_ffn_shw_down)
    identw = np.eye(128, dtype=np.float32)
    return dict(wqT=wqT, wkT=wkT, wvT=wvT, wuT=wuT, woT=woT, gwT=gwT,
                wge=wge, wue=wue, wde=wde, shwg=shwg, shwu=shwu, shwd=shwd,
                identw=identw)


def kernel(L0_attn_x, L0_attn_wq, L0_attn_wk, L0_attn_wv, L0_attn_wu,
           L0_attn_wo, L0_ffn_gate_w, L0_ffn_w_gate, L0_ffn_w_up,
           L0_ffn_w_down, L0_ffn_shgate_w, L0_ffn_shw_gate, L0_ffn_shw_up,
           L0_ffn_shw_down):
    x = np.asarray(L0_attn_x, dtype=np.float32)
    B, L, _ = x.shape
    ncores = 8
    T = B * L // ncores

    common = prep_inputs(L0_attn_wq, L0_attn_wk, L0_attn_wv, L0_attn_wu,
                         L0_attn_wo, L0_ffn_gate_w, L0_ffn_w_gate,
                         L0_ffn_w_up, L0_ffn_w_down, L0_ffn_shgate_w,
                         L0_ffn_shw_gate, L0_ffn_shw_up, L0_ffn_shw_down)
    halves_per_seq = L // T
    groups = [[s * halves_per_seq + j for j in range(halves_per_seq)]
              for s in range(B)]
    nc = build_program(T, groups)

    xf = x.reshape(ncores, T, D)
    in_maps = [dict(common, x=np.ascontiguousarray(xf[c]))
               for c in range(ncores)]
    res = run_bass_kernel_spmd(nc, in_maps, core_ids=list(range(ncores)))
    out = np.stack([res.results[c]["out"] for c in range(ncores)], axis=0)
    return out.reshape(B, L, D)


# version 15 (final)
# speedup vs baseline: 1.1881x; 1.1422x over previous
import numpy as np
from contextlib import ExitStack

import concourse.bass as bass
import concourse.mybir as mybir
from concourse.bass_utils import run_bass_kernel_spmd

F32 = mybir.dt.float32
BF16 = mybir.dt.bfloat16
AF = mybir.ActivationFunctionType
ALU = mybir.AluOpType
AX = mybir.AxisListType

D = 2048
KC = D // 128
HEADS = 16
HD = 128
NE = 8
H = 1408
HC = H // 128
HS = 2816
RSCALE = float(1.0 / np.sqrt(HD))
SRMS2 = 1.0 / HD
LN_EPS = 1e-5

ENGINES = ("tensor", "vector", "scalar", "gpsimd", "sync")


class KB:

    ENGINE_SEMS = ("pe", "v", "a")

    def __init__(self, nc, sems):
        self.nc = nc
        self.sems = sems
        self.cnt = {k: 0 for k in sems}
        self.ops = {e: [] for e in ENGINES}
        self.waited = {e: {} for e in ENGINES}
        self.max_waited = {}

    def op(self, eng, fn, deps=(), inc=None, step=1):
        lst = self.ops[eng]
        w = self.waited[eng]
        for d in deps:
            if d is None:
                continue
            s, v = d
            assert v <= self.cnt[s], (s, v, self.cnt[s])
            if w.get(s, 0) < v:
                sem = self.sems[s]
                lst.append(lambda E, sem=sem, v=v: E.wait_ge(sem, v))
                w[s] = v
                if v > self.max_waited.get(s, 0):
                    self.max_waited[s] = v
        if inc is None:
            lst.append(fn)
            return None
        self.cnt[inc] += step
        tok = (inc, self.cnt[inc])
        sem = self.sems[inc]
        lst.append(lambda E, fn=fn, sem=sem, step=step: fn(E).then_inc(sem, step))
        return tok

    def dma(self, eng, out, in_, deps=(), sem=None):
        return self.op(eng, lambda E: E.dma_start(out=out, in_=in_), deps,
                       inc=sem, step=16)

    def tok(self, s):
        return (s, self.cnt[s])

    def flush(self, skip=()):
        for s in self.cnt:
            if s not in self.ENGINE_SEMS and s not in skip and \
                    self.cnt[s] > self.max_waited.get(s, 0):
                self.op("sync", lambda E: E.nop(), deps=[(s, self.cnt[s])])
        ops = self.ops
        with self.nc.Block() as blk:
            for e in ENGINES:
                if ops[e]:
                    def runner(E, seq=tuple(ops[e])):
                        for f in seq:
                            f(E)
                    getattr(blk, e)(runner)
        self.ops = {e: [] for e in ENGINES}
        for e in ENGINES:
            prev = self.waited[e]
            nw = dict(self.cnt)
            for s in skip:
                nw[s] = prev.get(s, 0)
            self.waited[e] = nw
        nmw = dict(self.cnt)
        for s in skip:
            nmw[s] = self.max_waited.get(s, 0)
        self.max_waited = nmw


class PPool:

    def __init__(self, tensors):
        self.t = tensors
        self.cur = [0] * len(tensors)
        self.ctok = [[None] * n for (_, n) in tensors]
        self.rr = 0

    def claim(self, n):
        for _ in range(len(self.t)):
            ti = self.rr % len(self.t)
            self.rr += 1
            handle, nslots = self.t[ti]
            if nslots < n:
                continue
            if self.cur[ti] + n > nslots:
                self.cur[ti] = 0
            off = self.cur[ti]
            self.cur[ti] += n
            deps = [self.ctok[ti][off + j] for j in range(n)]
            return (ti, off, n, handle, deps)
        raise RuntimeError("no psum tensor big enough")

    def ap(self, grp, c0=0, c1=None):
        ti, off, n, handle, _ = grp
        base = off * 512
        if c1 is None:
            c1 = n * 512
        return handle[:, base + c0: base + c1]

    def release(self, grp, tok):
        ti, off, n, _, _ = grp
        for j in range(n):
            self.ctok[ti][off + j] = tok


STAT_NAMES = ("s1n", "s2", "mean_n", "msq", "ep", "lnv", "rstd", "nmr")


def _ln_xn(kb, xin_3d, xout_3d, nsubs, sq_scr, stats, x_tok, prev, war_out, cst):
    t_red = kb.op("vector", lambda E: E.reduce_sum(
        stats["s1n"], xin_3d, axis=AX.X, negate=True),
        deps=[x_tok, prev], inc="v")
    sq_tok = None
    for s in range(nsubs):
        sq_tok = kb.op("scalar", lambda E, s=s: E.activation(
            sq_scr[:, :], xin_3d[:, s, :], AF.Square,
            accum_out=stats["s2"][:, s: s + 1]),
            deps=[x_tok, prev] if s == 0 else [sq_tok], inc="a")
    t1 = kb.op("vector", lambda E: E.tensor_scalar(
        stats["mean_n"], stats["s1n"], 1.0 / D, None, ALU.mult),
        deps=[t_red], inc="v")
    t2 = kb.op("vector", lambda E: E.tensor_tensor(
        stats["msq"], stats["mean_n"], stats["mean_n"], ALU.mult),
        deps=[t1], inc="v")
    t3 = kb.op("vector", lambda E: E.tensor_scalar(
        stats["ep"], stats["s2"], 1.0 / D, None, ALU.mult),
        deps=[sq_tok], inc="v")
    t4 = kb.op("vector", lambda E: E.tensor_tensor(
        stats["ep"], stats["ep"], stats["msq"], ALU.subtract),
        deps=[t2, t3], inc="v")
    a1 = kb.op("scalar", lambda E: E.activation(
        stats["lnv"], stats["ep"], AF.Ln, bias=cst["eps"]), deps=[t4], inc="a")
    a2 = kb.op("scalar", lambda E: E.activation(
        stats["rstd"], stats["lnv"], AF.Exp, scale=cst["mhalf"]), deps=[a1], inc="a")
    t5 = kb.op("vector", lambda E: E.tensor_tensor(
        stats["nmr"], stats["mean_n"], stats["rstd"], ALU.mult),
        deps=[a2], inc="v")
    xn_tok = None
    for s in range(nsubs):
        xn_tok = kb.op("vector", lambda E, s=s: E.tensor_scalar(
            xout_3d[:, s, :], xin_3d[:, s, :],
            stats["rstd"][:, s: s + 1], stats["nmr"][:, s: s + 1],
            ALU.mult, ALU.add),
            deps=([t5, sq_tok] + list(war_out)) if s == 0 else [], inc="v")
    return xn_tok


def build_program(T, groups):
    nc = bass.Bass("TRN2", target_bir_lowering=False)
    NSUB = T // 128
    N1 = max(1, T // 512)
    P3T = 1024 if T % 1024 == 0 else 512
    N3 = T // P3T
    S3 = P3T // 128
    NT3 = P3T // 512

    x_d = nc.dram_tensor("x", [T, D], F32, kind="ExternalInput")
    wqT_d = nc.dram_tensor("wqT", [D, D], F32, kind="ExternalInput")
    wkT_d = nc.dram_tensor("wkT", [D, D], F32, kind="ExternalInput")
    wvT_d = nc.dram_tensor("wvT", [D, D], F32, kind="ExternalInput")
    wuT_d = nc.dram_tensor("wuT", [D, D], F32, kind="ExternalInput")
    woT_d = nc.dram_tensor("woT", [D, D], F32, kind="ExternalInput")
    gwT_d = nc.dram_tensor("gwT", [D, 16], F32, kind="ExternalInput")
    wge_d = nc.dram_tensor("wge", [NE, D, H], F32, kind="ExternalInput")
    wue_d = nc.dram_tensor("wue", [NE, D, H], F32, kind="ExternalInput")
    wde_d = nc.dram_tensor("wde", [NE, H, D], F32, kind="ExternalInput")
    shwg_d = nc.dram_tensor("shwg", [D, HS], F32, kind="ExternalInput")
    shwu_d = nc.dram_tensor("shwu", [D, HS], F32, kind="ExternalInput")
    shwd_d = nc.dram_tensor("shwd", [HS, D], F32, kind="ExternalInput")
    id_d = nc.dram_tensor("identw", [128, 128], F32, kind="ExternalInput")
    out_d = nc.dram_tensor("out", [T, D], F32, kind="ExternalOutput")
    kvp_d = nc.dram_tensor("kv_part", [128, D], F32, kind="Internal")
    kvf_d = nc.dram_tensor("kv_full", [128, D], F32, kind="Internal")
    xn_d = nc.dram_tensor("xn_sav", [N1, KC, 128, 512], BF16, kind="Internal")

    xr = x_d.ap().rearrange("(a p) d -> p a d", p=128)
    outr = out_d.ap().rearrange("(a p) d -> p a d", p=128)

    with ExitStack() as top:
        ec = top.enter_context
        sems = {}
        for s in ("pe", "v", "a", "scc", "swc",
                  "sx0", "sx1", "sw0", "sw1", "sw2", "sw3",
                  "so0", "so1", "sacc", "skv", "sxn", "sxl"):
            sems[s] = ec(nc.semaphore(s))
        kb = KB(nc, sems)

        ident = ec(nc.sbuf_tensor("ident", [128, 128], F32))
        c_all = ec(nc.sbuf_tensor("c_all", [128, NSUB, NE], F32))
        sg_all = ec(nc.sbuf_tensor("sg_all", [128, NSUB], F32))
        gw_sb = ec(nc.sbuf_tensor("gw_sb", [128, KC, 16], BF16))
        cst_t = ec(nc.sbuf_tensor("cst", [128, 8], F32))
        cst = {n: cst_t[:, i:i + 1] for i, n in enumerate(
            ("eps", "tiny", "mhalf", "srms2", "mone"))}

        with ExitStack() as p1:
            e1 = p1.enter_context
            xbuf = e1(nc.sbuf_tensor("p1_x", [128, 2, 4, D], F32))
            sq_scr = e1(nc.sbuf_tensor("p1_sq", [128, D], BF16))
            st_t = e1(nc.sbuf_tensor("p1_st", [128, 8, 4], F32))
            stt = {n: st_t[:, i, :] for i, n in enumerate(STAT_NAMES)}
            xn_sb = e1(nc.sbuf_tensor("p1_xn", [128, KC, 512], BF16))
            k_sb = e1(nc.sbuf_tensor("p1_k", [128, 4, D], BF16))
            v_sb = e1(nc.sbuf_tensor("p1_v", [128, 4, D], BF16))
            wbuf = e1(nc.sbuf_tensor("p1_w", [128, 2, 2, KC, 512], BF16))
            kvs = e1(nc.sbuf_tensor("p1_kvs", [128, D], F32))
            ppa = e1(nc.psum_tensor("p1_ppa", [128, 2048], F32))
            ppb = e1(nc.psum_tensor("p1_ppb", [128, 2048], F32))
            pool = PPool([(ppa, 4), (ppb, 4)])

            kv_zero = kb.op("vector", lambda E: E.memset(kvs[:, :], 0.0),
                            inc="v")
            cst_tok = None
            for ci, val in enumerate((LN_EPS, 1e-30, -0.5, SRMS2, -1.0)):
                cst_tok = kb.op("vector", lambda E, ci=ci, val=val: E.memset(
                    cst_t[:, ci:ci + 1], val), inc="v")
            kb.dma("gpsimd", ident[:, :], id_d[:, :], sem="swc")
            gwr = gwT_d.ap().rearrange("(k p) e -> p k e", p=128)
            gw_tok = kb.dma("gpsimd", gw_sb[:, :, :], gwr, sem="swc")
            ident_tok = gw_tok

            wkr = wkT_d.ap().rearrange("(k p) f -> p k f", p=128)
            wvr = wvT_d.ap().rearrange("(k p) f -> p k f", p=128)

            NFG = 4
            nload = N1 * NFG
            wl_tok = [None] * nload
            wl_war = [None] * nload

            def load_w1(j):
                fg = j % NFG
                par = j % 2
                dep = [wl_war[j - 2]] if j >= 2 else []
                kb.dma("gpsimd", wbuf[:, par, 0, :, :],
                       wkr[:, :, fg * 512:(fg + 1) * 512], deps=dep,
                       sem="sw%d" % par)
                wl_tok[j] = kb.dma("gpsimd", wbuf[:, par, 1, :, :],
                                   wvr[:, :, fg * 512:(fg + 1) * 512],
                                   sem="sw%d" % par)

            x_tok = [None] * N1
            x_free = [None] * N1

            def load_x1(i):
                dep = [x_free[i - 2]] if i >= 2 else []
                x_tok[i] = kb.dma("sync", xbuf[:, i % 2, :, :],
                                  xr[:, 4 * i:4 * i + 4, :], deps=dep,
                                  sem="sx%d" % (i % 2))

            load_x1(0)
            if N1 > 1:
                load_x1(1)
            load_w1(0)
            if nload > 1:
                load_w1(1)

            ln_tok = [None] * (N1 + 1)
            ln_tok[0] = _ln_xn(kb, xbuf[:, 0, :, :], xbuf[:, 0, :, :], 4,
                               sq_scr, stt, x_tok[0], cst_tok, [], cst)

            xn_store = None
            prev_xn_reader = None
            prev_kv_done = None
            kv_acc = [kv_zero] * 4
            for i in range(N1):
                par = i % 2
                xnf_tok = None
                for kc in range(KC):
                    g = pool.claim(1)
                    tr = None
                    for s in range(4):
                        dep = ([ln_tok[i], ident_tok] + g[4]) if s == 0 else []
                        tr = kb.op("tensor", lambda E, kc=kc, s=s, g=g, par=par: E.transpose(
                            pool.ap(g, s * 128, (s + 1) * 128),
                            xbuf[:, par, s, kc * 128:(kc + 1) * 128],
                            ident[:, :]), deps=dep,
                            inc="pe" if s == 3 else None)
                    xnf_tok = kb.op("scalar", lambda E, kc=kc, g=g: E.activation(
                        xn_sb[:, kc, :], pool.ap(g), AF.Copy),
                        deps=[tr, prev_xn_reader, xn_store], inc="a")
                    pool.release(g, xnf_tok)
                x_free[i] = kb.tok("pe")
                if i + 2 < N1:
                    load_x1(i + 2)
                xn_store = kb.dma(
                    "sync", xn_d.ap()[i].rearrange("k p t -> p k t"),
                    xn_sb[:, :, :], deps=[xnf_tok], sem="sxn")

                if i + 1 < N1:
                    ln_tok[i + 1] = _ln_xn(
                        kb, xbuf[:, (i + 1) % 2, :, :],
                        xbuf[:, (i + 1) % 2, :, :], 4,
                        sq_scr, stt, x_tok[i + 1], ln_tok[i], [], cst)

                evac_tok = None
                for fg in range(NFG):
                    j = i * NFG + fg
                    for m in range(2):
                        for s in range(4):
                            g = pool.claim(1)
                            mm = None
                            for kc in range(KC):
                                dep = ([wl_tok[j], xnf_tok] + g[4]) if kc == 0 else []
                                mm = kb.op("tensor", lambda E, kc=kc, s=s, m=m, g=g, par2=j % 2: E.matmul(
                                    pool.ap(g),
                                    xn_sb[:, kc, s * 128:(s + 1) * 128],
                                    wbuf[:, par2, m, kc, :],
                                    start=(kc == 0), stop=(kc == KC - 1)),
                                    deps=dep, inc="pe" if kc == KC - 1 else None)
                            if m == 0:
                                evac_tok = kb.op("vector", lambda E, s=s, fg=fg, g=g: E.tensor_scalar(
                                    k_sb[:, s, fg * 512:(fg + 1) * 512],
                                    pool.ap(g), 0.0, RSCALE, ALU.max, ALU.mult),
                                    deps=[mm, prev_kv_done], inc="v")
                            else:
                                evac_tok = kb.op("vector", lambda E, s=s, fg=fg, g=g: E.tensor_copy(
                                    v_sb[:, s, fg * 512:(fg + 1) * 512],
                                    pool.ap(g)),
                                    deps=[mm, prev_kv_done], inc="v")
                            pool.release(g, evac_tok)
                    wl_war[j] = kb.tok("pe")
                    if j + 2 < nload:
                        load_w1(j + 2)
                prev_xn_reader = kb.tok("pe")

                for hg in range(4):
                    g = pool.claim(1)
                    mm = None
                    for hq in range(4):
                        h = hg * 4 + hq
                        for s in range(4):
                            dep = ([evac_tok] + g[4]) if (hq == 0 and s == 0) else []
                            mm = kb.op("tensor", lambda E, s=s, h=h, hq=hq, g=g: E.matmul(
                                pool.ap(g, hq * 128, (hq + 1) * 128),
                                k_sb[:, s, h * 128:(h + 1) * 128],
                                v_sb[:, s, h * 128:(h + 1) * 128],
                                start=(s == 0), stop=(s == 3)),
                                deps=dep,
                                inc="pe" if (hq == 3 and s == 3) else None)
                    kv_acc[hg] = kb.op("vector", lambda E, hg=hg, g=g: E.tensor_tensor(
                        kvs[:, hg * 512:(hg + 1) * 512],
                        kvs[:, hg * 512:(hg + 1) * 512],
                        pool.ap(g), ALU.add),
                        deps=[mm, kv_acc[hg]], inc="v")
                    pool.release(g, kv_acc[hg])
                prev_kv_done = kb.tok("pe")

            t2 = kb.dma("scalar", kvp_d[:, :], kvs[:, :],
                        deps=[kv_acc[0], kv_acc[1], kv_acc[2], kv_acc[3]],
                        sem="skv")
            t3 = kb.op("gpsimd", lambda E: E.collective_compute(
                "AllReduce", ALU.add, replica_groups=groups,
                ins=[kvp_d.ap().opt()], outs=[kvf_d.ap().opt()]),
                deps=[t2], inc="scc")
            kb.flush(skip=("scc",))

        with ExitStack() as p2:
            e2 = p2.enter_context
            xbuf = e2(nc.sbuf_tensor("p2_x", [128, 2, 4, D], F32))
            o_sb = e2(nc.sbuf_tensor("p2_osb", [128, 4, D], F32))
            kv_bf = e2(nc.sbuf_tensor("p2_kvb", [128, D], BF16))
            xn_sb = e2(nc.sbuf_tensor("p2_xn", [128, KC, 512], BF16))
            u_sb = e2(nc.sbuf_tensor("p2_u", [128, 4, D], BF16))
            qof = e2(nc.sbuf_tensor("p2_qof", [128, 16, 512], BF16))
            wsl = e2(nc.sbuf_tensor("p2_w", [128, 3, KC, 512], BF16))
            log_sb = e2(nc.sbuf_tensor("p2_log", [128, 4, 16], F32))
            gsc = e2(nc.sbuf_tensor("p2_gsc", [128, 4, 32], F32))
            srms_scr = e2(nc.sbuf_tensor("p2_ss", [128, D], BF16))
            ssq = e2(nc.sbuf_tensor("p2_ssq", [128, 16], F32))
            rn = e2(nc.sbuf_tensor("p2_rn", [128, 16], F32))
            ppa = e2(nc.psum_tensor("p2_ppa", [128, 2048], F32))
            ppb = e2(nc.psum_tensor("p2_ppb", [128, 2048], F32))
            pool = PPool([(ppa, 4), (ppb, 4)])

            wqr = wqT_d.ap().rearrange("(k p) f -> p k f", p=128)
            wur = wuT_d.ap().rearrange("(k p) f -> p k f", p=128)
            wor = woT_d.ap().rearrange("(k p) f -> p k f", p=128)

            NSL = 12
            nload = N1 * NSL
            wl_tok = [None] * nload
            wl_war = [None] * nload

            def load_w2(j):
                sl = j % NSL
                par = j % 3
                src = (wqr, wur, wor)[sl // 4]
                fg = sl % 4
                dep = [wl_war[j - 3]] if j >= 3 else []
                wl_tok[j] = kb.dma("gpsimd", wsl[:, par, :, :],
                                   src[:, :, fg * 512:(fg + 1) * 512],
                                   deps=dep, sem="sw%d" % par)

            x_tok = [None] * N1
            x_free = [None] * N1

            def load_x2(i):
                dep = list(x_free[i - 2]) if i >= 2 else []
                x_tok[i] = kb.dma("sync", xbuf[:, i % 2, :, :],
                                  xr[:, 4 * i:4 * i + 4, :], deps=dep,
                                  sem="sx%d" % (i % 2))

            kv_rdy = None

            load_x2(0)
            if N1 > 1:
                load_x2(1)
            for j in range(3):
                load_w2(j)

            xn_ld = [None] * N1

            def load_xn2(i):
                dep = [prev_xn_reader]
                if i == 0:
                    dep.append(kb.tok("sxn"))
                xn_ld[i] = kb.dma("sync", xn_sb[:, :, :],
                                  xn_d.ap()[i].rearrange("k p t -> p k t"),
                                  deps=dep, sem="sxl")

            prev_xn_reader = None
            prev_gate = None
            prev_u_reader = None
            prev_qof_o_reader = None
            prev_obf_reader = None
            last_store = [None, None]
            load_xn2(0)
            for i in range(N1):
                par = i % 2
                xnf_tok = xn_ld[i]

                lg_evac = None
                for s in range(4):
                    g = pool.claim(1)
                    mm = None
                    for kc in range(KC):
                        dep = ([xnf_tok, gw_tok] + g[4]) if kc == 0 else []
                        mm = kb.op("tensor", lambda E, kc=kc, s=s, g=g: E.matmul(
                            pool.ap(g, 0, 16),
                            xn_sb[:, kc, s * 128:(s + 1) * 128],
                            gw_sb[:, kc, :],
                            start=(kc == 0), stop=(kc == KC - 1)),
                            deps=dep, inc="pe" if kc == KC - 1 else None)
                    lg_evac = kb.op("vector", lambda E, s=s, g=g: E.tensor_copy(
                        log_sb[:, s, :], pool.ap(g, 0, 16)),
                        deps=[mm, prev_gate], inc="v")
                    pool.release(g, lg_evac)
                e_v = gsc[:, :, 0:8]
                im_v = gsc[:, :, 8:16]
                e2_v = gsc[:, :, 16:24]
                m1_v = gsc[:, :, 24:25].squeeze(2)
                m2_v = gsc[:, :, 25:26].squeeze(2)
                den_v = gsc[:, :, 26:27].squeeze(2)
                ez_v = gsc[:, :, 27:28].squeeze(2)
                g1 = kb.op("scalar", lambda E: E.activation(
                    e_v, log_sb[:, :, 0:8], AF.Exp), deps=[lg_evac], inc="a")
                g2 = kb.op("vector", lambda E: E.reduce_max(
                    m1_v, e_v, axis=AX.X), deps=[g1], inc="v")
                g3 = kb.op("vector", lambda E: E.tensor_tensor(
                    im_v, e_v, m1_v.unsqueeze(2).broadcast_to([128, 4, 8]),
                    ALU.is_ge), deps=[g2], inc="v")
                g4 = kb.op("vector", lambda E: E.tensor_scalar(
                    im_v, im_v, -1.0, 1.0, ALU.mult, ALU.add), deps=[g3], inc="v")
                g5 = kb.op("vector", lambda E: E.tensor_tensor(
                    e2_v, e_v, im_v, ALU.mult), deps=[g4], inc="v")
                g6 = kb.op("vector", lambda E: E.reduce_max(
                    m2_v, e2_v, axis=AX.X), deps=[g5], inc="v")
                g7 = kb.op("vector", lambda E: E.tensor_tensor(
                    im_v, e_v, m2_v.unsqueeze(2).broadcast_to([128, 4, 8]),
                    ALU.is_ge), deps=[g6], inc="v")
                g8 = kb.op("vector", lambda E: E.tensor_tensor(
                    den_v, m1_v, m2_v, ALU.add), deps=[g6], inc="v")
                g9 = kb.op("vector", lambda E: E.reciprocal(
                    den_v, den_v), deps=[g8], inc="v")
                g10 = kb.op("vector", lambda E: E.tensor_tensor(
                    e2_v, e_v, im_v, ALU.mult), deps=[g7], inc="v")
                g11 = kb.op("vector", lambda E, i=i: E.tensor_tensor(
                    c_all[:, 4 * i:4 * i + 4, :], e2_v,
                    den_v.unsqueeze(2).broadcast_to([128, 4, 8]), ALU.mult),
                    deps=[g9, g10], inc="v")
                sg1 = kb.op("scalar", lambda E: E.activation(
                    ez_v.unsqueeze(2), log_sb[:, :, 8:9], AF.Exp, scale=cst["mone"]),
                    deps=[lg_evac], inc="a")
                sg2 = kb.op("vector", lambda E: E.tensor_scalar(
                    ez_v, ez_v, 1.0, None, ALU.add), deps=[sg1], inc="v")
                sg3 = kb.op("vector", lambda E, i=i: E.reciprocal(
                    sg_all[:, 4 * i:4 * i + 4], ez_v), deps=[sg2], inc="v")
                prev_gate = sg3

                q_evac = None
                for hg in range(4):
                    j = i * NSL + hg
                    g = pool.claim(4)
                    mm = None
                    for hq in range(4):
                        for kc in range(KC):
                            dep = ([wl_tok[j], xnf_tok, prev_qof_o_reader] + g[4]) \
                                if (hq == 0 and kc == 0) else []
                            mm = kb.op("tensor", lambda E, kc=kc, hq=hq, g=g, par2=j % 3: E.matmul(
                                pool.ap(g, hq * 512, (hq + 1) * 512),
                                wsl[:, par2, kc, hq * 128:(hq + 1) * 128],
                                xn_sb[:, kc, :],
                                start=(kc == 0), stop=(kc == KC - 1)),
                                deps=dep,
                                inc="pe" if (hq == 3 and kc == KC - 1) else None)
                    q_evac = kb.op("vector", lambda E, hg=hg, g=g: E.tensor_scalar(
                        qof[:, 4 * hg:4 * hg + 4, :].rearrange("p a b -> p (a b)"),
                        pool.ap(g), 0.0, RSCALE, ALU.max, ALU.mult),
                        deps=[mm], inc="v")
                    pool.release(g, q_evac)
                    wl_war[j] = kb.tok("pe")
                    if j + 3 < nload:
                        load_w2(j + 3)

                u_tok = None

                def emit_U():
                    nonlocal prev_xn_reader, u_tok
                    for fg in range(4):
                        j = i * NSL + 4 + fg
                        for s in range(4):
                            g = pool.claim(1)
                            mm = None
                            for kc in range(KC):
                                dep = ([wl_tok[j], xnf_tok, prev_u_reader] + g[4]) \
                                    if kc == 0 else []
                                mm = kb.op("tensor", lambda E, kc=kc, s=s, g=g, par2=j % 3: E.matmul(
                                    pool.ap(g),
                                    xn_sb[:, kc, s * 128:(s + 1) * 128],
                                    wsl[:, par2, kc, :],
                                    start=(kc == 0), stop=(kc == KC - 1)),
                                    deps=dep, inc="pe" if kc == KC - 1 else None)
                            u_tok = kb.op("scalar", lambda E, s=s, fg=fg, g=g: E.activation(
                                u_sb[:, s, fg * 512:(fg + 1) * 512], pool.ap(g),
                                AF.Silu), deps=[mm], inc="a")
                            pool.release(g, u_tok)
                        wl_war[j] = kb.tok("pe")
                        if j + 3 < nload:
                            load_w2(j + 3)
                    prev_xn_reader = kb.tok("pe")
                    if i + 1 < N1:
                        load_xn2(i + 1)

                def emit_O(fuse_mult):
                    toks = [None] * 4
                    prev_t1 = None
                    for s in range(4):
                        g = pool.claim(4)
                        mm = None
                        for h in range(HEADS):
                            dep = ([q_evac, kv_rdy] + g[4]) if h == 0 else []
                            mm = kb.op("tensor", lambda E, s=s, h=h, g=g: E.matmul(
                                pool.ap(g, h * 128, (h + 1) * 128),
                                qof[:, h, s * 128:(s + 1) * 128],
                                kv_bf[:, h * 128:(h + 1) * 128],
                                start=True, stop=True),
                                deps=dep, inc="pe" if h == HEADS - 1 else None)
                        o_cp = kb.op("vector", lambda E, s=s, g=g: E.tensor_copy(
                            o_sb[:, s, :], pool.ap(g)),
                            deps=[mm, prev_obf_reader, prev_u_reader], inc="v")
                        pool.release(g, o_cp)
                        o_sq = kb.op("scalar", lambda E, s=s: E.activation(
                            srms_scr[:, :], o_sb[:, s, :], AF.Square),
                            deps=[o_cp, prev_t1], inc="a")
                        o_red = kb.op("vector", lambda E: E.reduce_sum(
                            ssq[:, :],
                            srms_scr[:, :].rearrange("p (h e) -> p h e", e=128),
                            axis=AX.X), deps=[o_sq], inc="v")
                        o_ln = kb.op("scalar", lambda E: E.activation(
                            rn[:, :], ssq[:, :], AF.Ln, bias=cst["tiny"], scale=cst["srms2"]),
                            deps=[o_red], inc="a")
                        o_ex = kb.op("scalar", lambda E: E.activation(
                            rn[:, :], rn[:, :], AF.Exp, scale=cst["mhalf"]),
                            deps=[o_ln], inc="a")
                        o_mn = kb.op("vector", lambda E: E.tensor_scalar(
                            rn[:, :], rn[:, :], 1e12, None, ALU.min),
                            deps=[o_ex], inc="v")
                        t1 = kb.op("vector", lambda E, s=s: E.tensor_tensor(
                            o_sb[:, s, :].rearrange("p (h e) -> p h e", e=128),
                            o_sb[:, s, :].rearrange("p (h e) -> p h e", e=128),
                            rn[:, :].unsqueeze(2).broadcast_to([128, 16, 128]),
                            ALU.mult), deps=[o_mn], inc="v")
                        prev_t1 = t1
                        if fuse_mult:
                            toks[s] = kb.op("vector", lambda E, s=s: E.tensor_tensor(
                                o_sb[:, s, :], o_sb[:, s, :], u_sb[:, s, :],
                                ALU.mult),
                                deps=[t1, u_tok if s == 0 else None], inc="v")
                        else:
                            toks[s] = t1
                    return toks

                if i == 0:
                    emit_U()
                    kv_rdy = kb.dma("gpsimd", kv_bf[:, :], kvf_d[:, :],
                                    deps=[t3], sem="skv")
                    op_tok = emit_O(fuse_mult=True)
                else:
                    t1_tok = emit_O(fuse_mult=False)
                    emit_U()
                    op_tok = [None] * 4
                    for s in range(4):
                        op_tok[s] = kb.op("vector", lambda E, s=s: E.tensor_tensor(
                            o_sb[:, s, :], o_sb[:, s, :], u_sb[:, s, :],
                            ALU.mult),
                            deps=[t1_tok[s], u_tok if s == 0 else None], inc="v")
                prev_u_reader = op_tok[3]

                of_a = [None] * 4
                of_v = [None] * 4
                for s in range(4):
                    for jj in range(4):
                        g = pool.claim(1)
                        tr = None
                        for q4 in range(4):
                            kc = jj * 4 + q4
                            dep = ([op_tok[s]] + g[4]) if q4 == 0 else []
                            tr = kb.op("tensor", lambda E, kc=kc, s=s, q4=q4, g=g: E.transpose(
                                pool.ap(g, q4 * 128, (q4 + 1) * 128),
                                o_sb[:, s, kc * 128:(kc + 1) * 128],
                                ident[:, :]), deps=dep,
                                inc="pe" if q4 == 3 else None)
                        if jj % 2 == 0:
                            of_a[s] = kb.op("scalar", lambda E, s=s, jj=jj, g=g: E.activation(
                                qof[:, 4 * jj:4 * jj + 4, s * 128:(s + 1) * 128],
                                pool.ap(g).rearrange("p (a b) -> p a b", b=128),
                                AF.Copy), deps=[tr], inc="a")
                            pool.release(g, of_a[s])
                        else:
                            of_v[s] = kb.op("vector", lambda E, s=s, jj=jj, g=g: E.tensor_copy(
                                qof[:, 4 * jj:4 * jj + 4, s * 128:(s + 1) * 128],
                                pool.ap(g).rearrange("p (a b) -> p a b", b=128)),
                                deps=[tr], inc="v")
                            pool.release(g, of_v[s])
                prev_obf_reader = kb.tok("pe")

                ev = [[None] * 4 for _ in range(4)]
                for fh in range(4):
                    j = i * NSL + 8 + fh
                    for s in range(4):
                        g = pool.claim(1)
                        mm = None
                        for kc in range(KC):
                            dep = ([wl_tok[j], of_a[s], of_v[s]] + g[4]) \
                                if kc == 0 else []
                            mm = kb.op("tensor", lambda E, kc=kc, s=s, g=g, par2=j % 3: E.matmul(
                                pool.ap(g),
                                qof[:, kc, s * 128:(s + 1) * 128],
                                wsl[:, par2, kc, :],
                                start=(kc == 0), stop=(kc == KC - 1)),
                                deps=dep, inc="pe" if kc == KC - 1 else None)
                        ev[s][fh] = kb.op("vector", lambda E, s=s, fh=fh, g=g, par=par: E.tensor_tensor(
                            xbuf[:, par, s, fh * 512:(fh + 1) * 512],
                            pool.ap(g),
                            xbuf[:, par, s, fh * 512:(fh + 1) * 512],
                            ALU.add),
                            deps=[mm, x_tok[i]], inc="v")
                        pool.release(g, ev[s][fh])
                    wl_war[j] = kb.tok("pe")
                    if j + 3 < nload:
                        load_w2(j + 3)
                prev_qof_o_reader = kb.tok("pe")
                for s in range(4):
                    sp = s % 2
                    last_store[sp] = kb.dma(
                        "scalar", outr[:, 4 * i + s, :], xbuf[:, par, s, :],
                        deps=ev[s] + [last_store[sp]], sem="so%d" % sp)
                x_free[i] = (last_store[0], last_store[1])
                if i + 2 < N1:
                    load_x2(i + 2)

            kb.op("sync", lambda E: E.nop(),
                  deps=[last_store[0], last_store[1]])
            kb.flush()

        with ExitStack() as p3:
            e3 = p3.enter_context
            acc = e3(nc.sbuf_tensor("p3_acc", [128, S3, D], F32))
            xn_sb = e3(nc.sbuf_tensor("p3_xn", [128, KC, P3T], BF16))
            wd_sb = e3(nc.sbuf_tensor("p3_wd", [128, HC * D], BF16))
            h_sb = e3(nc.sbuf_tensor("p3_h", [128, HC, P3T], BF16))
            sgc = e3(nc.sbuf_tensor("p3_sgc", [128, 2, P3T], BF16))
            wgu = e3(nc.sbuf_tensor("p3_wgu", [128, 4, 2, KC, 128], BF16))
            ppa = e3(nc.psum_tensor("p3_ppa", [128, 2048], F32))
            ppb = e3(nc.psum_tensor("p3_ppb", [128, 2048], F32))
            pool = PPool([(ppa, 4), (ppb, 4)])

            wge_r = [wge_d.ap()[e].rearrange("(k p) h -> p k h", p=128)
                     for e in range(NE)]
            wue_r = [wue_d.ap()[e].rearrange("(k p) h -> p k h", p=128)
                     for e in range(NE)]
            wde_r = [wde_d.ap()[e].rearrange("(c p) d -> p c d", p=128)
                     for e in range(NE)]
            shwg_r = shwg_d.ap().rearrange("(k p) h -> p k h", p=128)
            shwu_r = shwu_d.ap().rearrange("(k p) h -> p k h", p=128)
            shwd_r = shwd_d.ap().rearrange("(c p) d -> p c d", p=128)

            def g_src(e, hc):
                if e < NE:
                    return wge_r[e][:, :, hc * 128:(hc + 1) * 128]
                off = (e - NE) * H + hc * 128
                return shwg_r[:, :, off:off + 128]

            def u_src(e, hc):
                if e < NE:
                    return wue_r[e][:, :, hc * 128:(hc + 1) * 128]
                off = (e - NE) * H + hc * 128
                return shwu_r[:, :, off:off + 128]

            def d_src(e):
                if e < NE:
                    return wde_r[e]
                return shwd_r[:, (e - NE) * HC:(e - NE + 1) * HC, :]

            NEX = NE + 2

            def load_xn3(q, dep):
                t = None
                for j in range(NT3):
                    t = kb.dma("sync",
                               xn_sb[:, :, j * 512:(j + 1) * 512],
                               xn_d.ap()[NT3 * q + j].rearrange("k p t -> p k t"),
                               deps=[dep] if j == 0 else [], sem="sxl")
                return t

            acc_store = None
            prev_dn_pe = None
            prev_g_pe = [None] * 4
            prev_h = [None, None]
            prev_stt = {}
            xn_next = load_xn3(0, None)
            for q in range(N3):
                xn_ready = xn_next
                acc_ld = kb.dma("sync", acc[:, :, :],
                                outr[:, q * S3:(q + 1) * S3, :],
                                deps=[acc_store], sem="sacc")

                for e in range(NEX):
                    gu_tok = [None] * HC
                    wd_tok = [None] * HC

                    def load_gu(hc, e=e, gu_tok=gu_tok):
                        par = hc % 4
                        dep = [prev_g_pe[par]]
                        kb.dma("gpsimd", wgu[:, par, 0, :, :], g_src(e, hc),
                               deps=dep, sem="sw%d" % par)
                        gu_tok[hc] = kb.dma("gpsimd", wgu[:, par, 1, :, :],
                                            u_src(e, hc), sem="sw%d" % par)

                    def load_wd(hc, e=e, wd_tok=wd_tok):
                        dep = [prev_dn_pe] if hc == 0 else []
                        wd_tok[hc] = kb.dma(
                            "gpsimd",
                            wd_sb[:, hc * D:(hc + 1) * D],
                            d_src(e)[:, hc, :],
                            deps=dep, sem="swc")

                    for hc in range(min(4, HC)):
                        load_gu(hc)
                    load_wd(0)
                    h_tok = None
                    for hc in range(HC):
                        par = hc % 4
                        sp = hc % 2
                        g = pool.claim(NT3)
                        mm = None
                        for n in range(NT3):
                            for kc in range(KC):
                                dep = ([gu_tok[hc], xn_ready, prev_h[sp]] + g[4]) \
                                    if (kc == 0 and n == 0) else []
                                mm = kb.op("tensor", lambda E, kc=kc, n=n, g=g, par=par: E.matmul(
                                    pool.ap(g, n * 512, (n + 1) * 512),
                                    wgu[:, par, 0, kc, :],
                                    xn_sb[:, kc, n * 512:(n + 1) * 512],
                                    start=(kc == 0), stop=(kc == KC - 1)),
                                    deps=dep,
                                    inc="pe" if (kc == KC - 1 and n == NT3 - 1) else None)
                        sil = kb.op("scalar", lambda E, sp=sp, g=g: E.activation(
                            sgc[:, sp, :], pool.ap(g), AF.Silu),
                            deps=[mm, prev_h[sp]], inc="a")
                        pool.release(g, sil)
                        g2 = pool.claim(NT3)
                        mm = None
                        for n in range(NT3):
                            for kc in range(KC):
                                dep = ([gu_tok[hc]] + g2[4]) if (kc == 0 and n == 0) else []
                                mm = kb.op("tensor", lambda E, kc=kc, n=n, g=g2, par=par: E.matmul(
                                    pool.ap(g, n * 512, (n + 1) * 512),
                                    wgu[:, par, 1, kc, :],
                                    xn_sb[:, kc, n * 512:(n + 1) * 512],
                                    start=(kc == 0), stop=(kc == KC - 1)),
                                    deps=dep,
                                    inc="pe" if (kc == KC - 1 and n == NT3 - 1) else None)
                        prev_g_pe[par] = kb.tok("pe")
                        if hc + 4 < HC:
                            load_gu(hc + 4)
                        if hc + 1 < HC:
                            load_wd(hc + 1)
                        h_tok = kb.op("vector", lambda E, hc=hc, g=g2, sp=sp: E.tensor_tensor(
                            h_sb[:, hc, :], pool.ap(g), sgc[:, sp, :],
                            ALU.mult), deps=[mm, sil], inc="v")
                        pool.release(g2, h_tok)
                        prev_h[sp] = h_tok
                        if e == NEX - 1 and hc == HC - 1 and q + 1 < N3:
                            xn_next = load_xn3(q + 1, kb.tok("pe"))
                    for s in range(S3):
                        sub = q * S3 + s
                        if e < NE:
                            c_scal = c_all[:, sub, e:e + 1]
                        else:
                            c_scal = sg_all[:, sub:sub + 1]
                        for fp in range(2):
                            g = pool.claim(2)
                            mm = None
                            for fh in range(2):
                                for hc in range(HC):
                                    dep = ([h_tok, wd_tok[HC - 1], acc_ld] + g[4]) \
                                        if (hc == 0 and fh == 0) else []
                                    mm = kb.op("tensor", lambda E, hc=hc, fh=fh, s=s, fp=fp, g=g: E.matmul(
                                        pool.ap(g, fh * 512, (fh + 1) * 512),
                                        h_sb[:, hc, s * 128:(s + 1) * 128],
                                        wd_sb[:, hc * D + fp * 1024 + fh * 512:
                                              hc * D + fp * 1024 + (fh + 1) * 512],
                                        start=(hc == 0), stop=(hc == HC - 1)),
                                        deps=dep,
                                        inc="pe" if (hc == HC - 1 and fh == 1) else None)
                            stt = kb.op("vector", lambda E, s=s, fp=fp, g=g, c_scal=c_scal: E.scalar_tensor_tensor(
                                acc[:, s, fp * 1024:(fp + 1) * 1024],
                                pool.ap(g), c_scal,
                                acc[:, s, fp * 1024:(fp + 1) * 1024],
                                ALU.mult, ALU.add),
                                deps=[mm, acc_ld, prev_stt.get((s, fp))], inc="v")
                            prev_stt[(s, fp)] = stt
                            pool.release(g, stt)
                    prev_dn_pe = kb.tok("pe")
                st_deps = [prev_stt[(s, fp)] for s in range(S3) for fp in range(2)]
                acc_store = kb.dma("scalar", outr[:, q * S3:(q + 1) * S3, :],
                                   acc[:, :, :], deps=st_deps + [acc_store],
                                   sem="sacc")
            kb.op("sync", lambda E: E.nop(), deps=[acc_store])
            kb.flush()

    nc.finalize()
    return nc


def prep_inputs(L0_attn_wq, L0_attn_wk, L0_attn_wv, L0_attn_wu, L0_attn_wo,
                L0_ffn_gate_w, L0_ffn_w_gate, L0_ffn_w_up, L0_ffn_w_down,
                L0_ffn_shgate_w, L0_ffn_shw_gate, L0_ffn_shw_up,
                L0_ffn_shw_down):
    f32c = lambda a: np.ascontiguousarray(np.asarray(a, dtype=np.float32))
    wqT = f32c(np.asarray(L0_attn_wq).T)
    wkT = f32c(np.asarray(L0_attn_wk).T)
    wvT = f32c(np.asarray(L0_attn_wv).T)
    wuT = f32c(np.asarray(L0_attn_wu).T)
    woT = f32c(np.asarray(L0_attn_wo).T)
    gwT = np.zeros((D, 16), dtype=np.float32)
    gwT[:, 0:8] = np.asarray(L0_ffn_gate_w, dtype=np.float32).T
    gwT[:, 8] = np.asarray(L0_ffn_shgate_w, dtype=np.float32)
    wge = f32c(L0_ffn_w_gate)
    wue = f32c(L0_ffn_w_up)
    wde = f32c(L0_ffn_w_down)
    shwg = f32c(L0_ffn_shw_gate)
    shwu = f32c(L0_ffn_shw_up)
    shwd = f32c(L0_ffn_shw_down)
    identw = np.eye(128, dtype=np.float32)
    return dict(wqT=wqT, wkT=wkT, wvT=wvT, wuT=wuT, woT=woT, gwT=gwT,
                wge=wge, wue=wue, wde=wde, shwg=shwg, shwu=shwu, shwd=shwd,
                identw=identw)


def kernel(L0_attn_x, L0_attn_wq, L0_attn_wk, L0_attn_wv, L0_attn_wu,
           L0_attn_wo, L0_ffn_gate_w, L0_ffn_w_gate, L0_ffn_w_up,
           L0_ffn_w_down, L0_ffn_shgate_w, L0_ffn_shw_gate, L0_ffn_shw_up,
           L0_ffn_shw_down):
    x = np.asarray(L0_attn_x, dtype=np.float32)
    B, L, _ = x.shape
    ncores = 8
    T = B * L // ncores

    common = prep_inputs(L0_attn_wq, L0_attn_wk, L0_attn_wv, L0_attn_wu,
                         L0_attn_wo, L0_ffn_gate_w, L0_ffn_w_gate,
                         L0_ffn_w_up, L0_ffn_w_down, L0_ffn_shgate_w,
                         L0_ffn_shw_gate, L0_ffn_shw_up, L0_ffn_shw_down)
    halves_per_seq = L // T
    groups = [[s * halves_per_seq + j for j in range(halves_per_seq)]
              for s in range(B)]
    nc = build_program(T, groups)

    xf = x.reshape(ncores, T, D)
    in_maps = [dict(common, x=np.ascontiguousarray(xf[c]))
               for c in range(ncores)]
    res = run_bass_kernel_spmd(nc, in_maps, core_ids=list(range(ncores)))
    out = np.stack([res.results[c]["out"] for c in range(ncores)], axis=0)
    return out.reshape(B, L, D)
